# version 17
# baseline (speedup 1.0000x reference)
import math
import os
import numpy as np
import contextlib
import concourse.bass as bass
import concourse.mybir as mybir
from concourse.bass_utils import run_bass_kernel_spmd

F32 = mybir.dt.float32
BF16 = mybir.dt.bfloat16
I32 = mybir.dt.int32
AF = mybir.ActivationFunctionType
ALU = mybir.AluOpType
AX = mybir.AxisListType.X

CD = 1024
E = 8
EPS = 1e-5
GN_EPS = 1e-5
G_CENTER = 1.0 / math.e
G_LOWER = 0.5 - math.log(4.0 / 3.0)
G_UPPER = 0.5
MAGIC = 0x5F3759DF


class Dummy:
    def then_inc(self, *a, **k):
        return self


class EngProxy:
    def __getattr__(self, name):
        return lambda *a, **k: Dummy()


SEM_NAMES = ["XA0", "XA1", "WL0", "WL1", "LD0", "LD1", "ST0", "ST1",
             "SS0", "SS1", "PE", "DV", "AC"]


class Sched:
    def __init__(self):
        self.dry = True
        self.lab = {}
        self.counts = {}
        self.sems = {}
        self.dcount = {}

    def reset(self):
        self.counts = {k: 0 for k in SEM_NAMES}
        self.dcount = {k: 0 for k in ["XA", "WL", "LD", "ST", "SS"]}

    def dma(self, eng, stream, out, in_, label=None):
        idx = self.dcount[stream]
        self.dcount[stream] += 1
        snm = stream + str(idx % 2)
        prev = self.counts[snm]
        if not self.dry and prev > 0:
            eng.wait_ge(self.sems[snm], prev)
        ii = eng.dma_start(out=out, in_=in_)
        return self.inc(ii, snm, label, 16)

    def dmaT(self, eng, stream, out, in_, label=None):
        idx = self.dcount[stream]
        self.dcount[stream] += 1
        snm = stream + str(idx % 2)
        prev = self.counts[snm]
        if not self.dry and prev > 0:
            eng.wait_ge(self.sems[snm], prev)
        ii = eng.dma_start_transpose(out, in_)
        return self.inc(ii, snm, label, 16)

    def inc(self, instr, sem, label=None, n=1):
        self.counts[sem] += n
        if self.dry:
            if label is not None:
                assert label not in self.lab, f"dup label {label}"
                self.lab[label] = (sem, self.counts[sem])
        else:
            instr.then_inc(self.sems[sem], n)
        return instr

    def alias(self, new, old):
        if self.dry:
            self.lab[new] = self.lab[old]

    def wait(self, eng, label):
        if self.dry:
            return
        sem, v = self.lab[label]
        eng.wait_ge(self.sems[sem], v)

    def wait_total(self, eng, sem):
        if not self.dry:
            eng.wait_ge(self.sems[sem], self.counts[sem])


def build_program(N, stop=6):
    SUB = int(os.environ.get('KSUB', '4'))
    _EXPF = AF.Relu if os.environ.get('KEXP') == '0' else AF.Exp
    S3 = int(os.environ.get('KS3', '5'))
    S5 = int(os.environ.get('KS5', '3'))
    NCH = N // 128
    W = min(1024, N)
    NQ = N // W
    WCH = W // 128
    VT = min(1024, N)
    NVT = N // VT
    NV = 8 * NVT
    NC5 = N // 512

    nc = bass.Bass("TRN2", target_bir_lowering=False)

    d = {}
    d["x"] = nc.dram_tensor("x", [N, CD], F32, kind="ExternalInput").ap()
    d["wqk"] = nc.dram_tensor("wqk", [128, 8, 2048], F32, kind="ExternalInput").ap()
    d["wv"] = nc.dram_tensor("wv", [128, 8192], F32, kind="ExternalInput").ap()
    d["projw"] = nc.dram_tensor("projw", [128, 8, 1024], F32, kind="ExternalInput").ap()
    d["projb"] = nc.dram_tensor("projb", [1, 1024], F32, kind="ExternalInput").ap()
    d["wr"] = nc.dram_tensor("wr", [128, 64], F32, kind="ExternalInput").ap()
    d["chw1"] = nc.dram_tensor("chw1", [128, 2048], F32, kind="ExternalInput").ap()
    d["chw2"] = nc.dram_tensor("chw2", [128, 2048], F32, kind="ExternalInput").ap()
    d["chb1"] = nc.dram_tensor("chb1", [128, 2], F32, kind="ExternalInput").ap()
    d["chb2"] = nc.dram_tensor("chb2", [128, 8], F32, kind="ExternalInput").ap()
    d["conv1"] = nc.dram_tensor("conv1", [1, 12], F32, kind="ExternalInput").ap()
    d["conv2"] = nc.dram_tensor("conv2", [4, 3], F32, kind="ExternalInput").ap()
    d["gng"] = nc.dram_tensor("gng", [4, 1], F32, kind="ExternalInput").ap()
    d["gnb"] = nc.dram_tensor("gnb", [4, 1], F32, kind="ExternalInput").ap()
    d["scal4"] = nc.dram_tensor("scal4", [1, 4], F32, kind="ExternalInput").ap()
    d["eye"] = nc.dram_tensor("eye", [128, 128], F32, kind="ExternalInput").ap()
    d["sel42d"] = nc.dram_tensor("sel42d", [4, 2], F32, kind="ExternalInput").ap()
    d["sel24d"] = nc.dram_tensor("sel24d", [2, 4], F32, kind="ExternalInput").ap()
    for nm in ("wg", "wu", "wd"):
        d[nm] = nc.dram_tensor(nm, [E, 128, 8192], F32, kind="ExternalInput").ap()
    d["out"] = nc.dram_tensor("out", [N, CD], F32, kind="ExternalOutput").ap()
    d["sc_x1tm"] = nc.dram_tensor("sc_x1tm", [N, CD], F32, kind="Internal").ap()
    d["sc_x1bf"] = nc.dram_tensor("sc_x1bf", [N, CD], BF16, kind="Internal").ap()
    d["sc_row"] = nc.dram_tensor("sc_row", [N], F32, kind="Internal").ap()
    d["sc_wr"] = nc.dram_tensor("sc_wr", [8, N], F32, kind="Internal").ap()

    TOP = nc.sbuf_top
    base = ((nc.sbuf_base + 511) // 512) * 512
    sb = {}

    class Bump:
        def __init__(self, start):
            self.o = start

        def take(self, nbytes):
            r = self.o
            self.o += ((nbytes + 31) // 32) * 32
            return r

    bump = Bump(base)

    def alloc(name, shape, dtype, at=None):
        nbytes = int(np.prod(shape[1:])) * (2 if dtype == BF16 else 4)
        o = bump.take(nbytes) if at is None else at
        t = nc.alloc_sbuf_tensor_at(name, list(shape), dtype, offset=o,
                                    align_bytes=32)
        sb[name] = t.ap()
        return sb[name]

    alloc("ident_bf", [128, 128], BF16)
    alloc("ident_f", [128, 128], F32)
    alloc("ones_r_bf", [1, 128], BF16)
    alloc("ones_r_f", [1, 128], F32)
    alloc("ones_c_f", [128, 1], F32)
    alloc("muq", [128, NCH], F32)
    alloc("varq", [128, NCH], F32)
    alloc("muk", [128, NCH], F32)
    alloc("vark", [128, NCH], F32)
    alloc("sqk", [128, NCH], F32)
    alloc("a_tm", [128, NCH], F32)
    alloc("w_col", [128, NCH], F32)
    alloc("st6", [128, 12], F32)
    alloc("mv2", [128, 2], F32)
    alloc("tn1", [128, 4], F32)
    alloc("tn2", [128, 4], F32)
    alloc("consts", [128, 8], F32)
    alloc("m_fm", [128, 8], F32)
    alloc("m_bf", [128, 8], BF16)
    alloc("h1ch", [128, 2], BF16)
    alloc("ch_w", [128, 8], F32)
    alloc("negb2", [128, 8], F32)
    alloc("chb1_sb", [128, 2], F32)
    alloc("chb2_sb", [128, 8], F32)
    alloc("scal4_sb", [1, 4], F32)
    alloc("conv1_sb", [1, 12], F32)
    alloc("conv2_sb", [4, 3], F32)
    alloc("conv1_bf", [1, 12], BF16)
    alloc("conv2_bf", [4, 3], BF16)
    alloc("gng_sb", [4, 1], F32)
    alloc("gnb_sb", [4, 1], F32)
    alloc("sel42", [4, 2], F32)
    alloc("sel24", [2, 4], F32)
    alloc("gn_sc", [4, 56], F32)
    alloc("gtmp", [4, 8], F32)
    alloc("arow32", [NCH, 128], F32)
    alloc("wr_f", [128, 64], F32)
    alloc("wr_bf", [128, 64], BF16)
    alloc("wroute", [128, NCH * 8], F32)
    alloc("rt4", [128, NCH], F32)
    alloc("rt5", [128, NCH], F32)
    pbbc = bump.take(4096)
    alloc("projb_bc", [128, 1024], F32, at=pbbc)
    alloc("wbc", [128, W], F32, at=pbbc)

    SCR = bump.take(16896)
    alloc("xa0", [128, 1024], BF16, at=SCR)
    alloc("xa1", [128, 1024], BF16, at=SCR + 2048)
    alloc("q_sb", [128, 1024], F32, at=SCR + 4096)
    alloc("rt1", [128, NCH * 8], F32, at=SCR + 4096)
    alloc("qk_sc", [128, 1024], F32, at=SCR + 8192)
    alloc("rt3", [128, NCH * 8], F32, at=SCR + 8192)
    alloc("projb_row", [1, 1024], F32, at=SCR + 12288)
    alloc("rt2", [128, NCH * 8], F32, at=SCR + 12288)
    alloc("arow", [1, N + 8], F32, at=SCR)
    alloc("w_rows", [8, N], F32, at=SCR)

    ded = bump.take(4096)
    alloc("zbuf", [128, 1024], F32, at=ded)
    ded2 = bump.take(4096)
    alloc("x1b", [128, 1024], F32, at=ded2)
    alloc("obb", [128, 1024], F32, at=ded2)
    alloc("hib", [128, 1024], BF16, at=SCR + 8192)
    alloc("hib1", [128, 1024], BF16, at=SCR + 10240)
    ded4 = bump.take(4096)
    alloc("xlb", [128, 1024], F32, at=ded4)
    alloc("x1b1", [128, 1024], F32, at=ded4)
    ded5 = bump.take(4096)
    alloc("xfb0", [128, 1024], F32, at=ded5)
    alloc("wrow0", [1, 1024], F32, at=ded5)
    ded6 = bump.take(4096)
    alloc("xfb1", [128, 1024], F32, at=ded6)
    alloc("wrowb0", [1, 1024], BF16, at=ded6)
    alloc("wrowb1", [1, 1024], BF16, at=ded6 + 2048)

    BB = ((bump.o + 511) // 512) * 512
    K = 1024
    if N == 4096:
        alloc("x_fm", [128, 8, N], BF16, at=BB)
        alloc("slot0", [128, 8192], BF16, at=BB)
        alloc("chw1_sb", [128, 2048], BF16, at=BB)
        alloc("chw2_sb", [128, 2048], BF16, at=BB + 4096)
        alloc("slot1", [128, 8192], BF16, at=BB + 16 * K)
        alloc("pp_sb", [128, 8, 1024], BF16, at=BB + 32 * K)
        alloc("slot2", [128, 8192], BF16, at=BB + 32 * K)
        alloc("x1q", [128, 8, W], BF16, at=BB + 48 * K)
        alloc("pstg32", [128, 8, 1024], F32, at=BB + 48 * K)
        alloc("wqk_sb", [128, 8, 2048], BF16, at=BB + 64 * K)
        alloc("wv_sb", [128, 8192], BF16, at=BB + 64 * K)
        alloc("hbuf", [128, 8, W], BF16, at=BB + 64 * K)
        alloc("sil0", [128, W], BF16, at=BB + 80 * K)
        alloc("sil1", [128, W], BF16, at=BB + 80 * K + 2 * W)
        alloc("upw0", [128, W], BF16, at=BB + 80 * K + 4 * W)
        alloc("upw1", [128, W], BF16, at=BB + 80 * K + 6 * W)
        alloc("v_fm", [128, 8, N], BF16, at=BB + 80 * K)
        alloc("acc0", [128, 8, W], F32, at=BB + 88 * K)
        alloc("acc1", [128, 8, W], F32, at=BB + 88 * K + 32 * W)
        alloc("h4", [4, N + 8], F32, at=BB + 80 * K + 16 * N)
        alloc("arow_bf", [1, N + 2], BF16, at=BB + 88 * K)
        alloc("h4bf", [4, N + 2], BF16, at=BB + 88 * K + 8256)
        top_need = max(BB + 80 * K + 16 * N + 4 * (N + 8),
                       BB + 88 * K + 64 * W)
    else:
        o = BB
        alloc("x_fm", [128, 8, N], BF16, at=o); o += 16 * N
        wqk_at = o
        alloc("wqk_sb", [128, 8, 2048], BF16, at=o); o += 32 * K
        alloc("h4", [4, N + 8], F32, at=wqk_at)
        alloc("h4bf", [4, N + 2], BF16, at=wqk_at + 12 * K)
        alloc("arow_bf", [1, N + 2], BF16, at=wqk_at + 18 * K)
        alloc("sil0", [128, W], BF16, at=wqk_at + 4 * K)
        alloc("sil1", [128, W], BF16, at=wqk_at + 4 * K + 2 * W)
        alloc("upw0", [128, W], BF16, at=wqk_at + 4 * K + 4 * W)
        alloc("upw1", [128, W], BF16, at=wqk_at + 4 * K + 6 * W)
        alloc("wv_sb", [128, 8192], BF16, at=o)
        alloc("hbuf", [128, 8, W], BF16, at=o); o += 16 * K
        alloc("v_fm", [128, 8, N], BF16, at=o); o += 16 * N
        alloc("slot0", [128, 8192], BF16, at=o)
        alloc("chw1_sb", [128, 2048], BF16, at=o)
        alloc("chw2_sb", [128, 2048], BF16, at=o + 4096)
        o += 16 * K
        alloc("slot1", [128, 8192], BF16, at=o); o += 16 * K
        alloc("pp_sb", [128, 8, 1024], BF16, at=o)
        alloc("slot2", [128, 8192], BF16, at=o); o += 16 * K
        alloc("x1q", [128, 8, W], BF16, at=o); o += 2 * 8 * W
        alloc("pstg32", [128, 8, 1024], F32, at=o)
        alloc("acc0", [128, 8, W], F32, at=o); o += 32 * W
        alloc("acc1", [128, 8, W], F32, at=o); o += 32 * W
        top_need = max(o, o - 64 * W + 32 * 1024)
    assert top_need <= TOP, f"SBUF overflow: need {top_need}, top {TOP}"

    stack = contextlib.ExitStack()
    PSF = stack.enter_context(nc.psum_tensor("PSF", [128, 3072], F32))
    PSB = stack.enter_context(nc.psum_tensor("PSB", [128, 2048], BF16))
    psf = PSF[:, :]
    psb = PSB[:, :]
    psb32 = PSB[:, :].bitcast(F32)

    S = Sched()
    for snm in SEM_NAMES:
        S.sems[snm] = stack.enter_context(nc.semaphore(snm))
    S.reset()

    sl = [sb["slot0"], sb["slot1"], sb["slot2"]]
    silb = [sb["sil0"], sb["sil1"]]
    upwb = [sb["upw0"], sb["upw1"]]
    accb = [sb["acc0"], sb["acc1"]]
    xtmp = [sb["xa0"], sb["xa1"]]
    xfb = [sb["xfb0"], sb["xfb1"]]
    x1bb = [sb["x1b"], sb["x1b1"]]
    hibb = [sb["hib"], sb["hib1"]]


    def em_gpsimd(g):
        S.dma(g, "WL", sb["ident_bf"], d["eye"], "identbf")
        for p4 in range(4):
            S.dma(g, "WL", sb["wqk_sb"][:, 2 * p4:2 * p4 + 2, :],
                  d["wqk"][:, 2 * p4:2 * p4 + 2, :], f"wqk{p4}")
        for i in range(NCH):
            if i >= 2:
                S.wait(g, f"tr{i - 2}")
            S.dma(g, "XA", xtmp[i % 2], d["x"][i * 128:(i + 1) * 128, :], f"xa{i}")
        if stop < 2:
            return
        S.wait(g, f"qk{NCH - 1}")
        S.dma(g, "WL", sb["wv_sb"], d["wv"], "wv")
        if stop < 3 or (stop == 3 and S3 < 2):
            return
        S.wait(g, "arwr")
        S.wait(g, f"v{NV - 1}")
        S.dma(g, "WL", sb["arow_bf"][0:1, 1:N + 1], d["sc_row"][:], "arbf")
        if stop == 3 and (S3 < 5 or S5 < 2):
            return
        S.wait(g, f"v{NV - 1}")
        S.wait(g, "mfm")
        S.dma(g, "WL", sb["chw1_sb"], d["chw1"], "chw1")
        S.dma(g, "WL", sb["chw2_sb"], d["chw2"], "chw2")
        if stop < 6:
            return
        wsrc = {0: d["wg"], 1: d["wu"], 2: d["wd"]}
        for q in range(NQ):
            for e in range(E):
                for m in range(3):
                    j = (q * E + e) * 3 + m
                    if j == 0:
                        S.wait(g, "ch2_7")
                    if j == 2:
                        S.wait(g, f"proj{NCH - 1}")
                    if j >= 3:
                        S.wait(g, f"mmdone{j - 3}")
                    S.dma(g, "WL", sl[j % 3], wsrc[m][e], f"slot{j}")

    def em_sync(s):
        S.dma(s, "LD", sb["ident_f"], d["eye"])
        for src, dst in [(d["sel42d"], sb["sel42"]), (d["sel24d"], sb["sel24"]),
                         (d["scal4"], sb["scal4_sb"]), (d["conv1"], sb["conv1_sb"]),
                         (d["conv2"], sb["conv2_sb"]), (d["gng"], sb["gng_sb"]),
                         (d["gnb"], sb["gnb_sb"]), (d["chb1"], sb["chb1_sb"]),
                         (d["chb2"], sb["chb2_sb"])]:
            S.dma(s, "LD", dst, src)
        S.dma(s, "LD", sb["wr_f"], d["wr"], "consts2")
        S.dma(s, "LD", sb["projb_row"], d["projb"], "consts")
        if stop < 3:
            return
        if stop == 3 and S3 < 2:
            return
        S.wait(s, "arwr")
        S.dma(s, "LD", sb["arow"][0:1, 1:N + 1], d["sc_row"][:], "arrd")
        if stop == 3 and S3 < 4:
            return
        S.wait(s, "a1wr")
        S.dma(s, "LD", sb["arow32"],
              d["sc_row"].rearrange("(a b) -> a b", b=128), "a1rd")
        if stop == 3 and (S3 < 5 or S5 < 3):
            return
        S.wait(s, f"v{NV - 1}")
        S.dma(s, "LD", sb["pstg32"], d["projw"], "pjall")
        if stop < 4:
            return
        def em_bstores(i):
            S.wait(s, f"x1{i}")
            S.dma(s, "SS", d["sc_x1tm"][i * 128:(i + 1) * 128, :], x1bb[i % 2],
                  f"x1st{i}")
            S.wait(s, f"xbf{i}")
            S.dma(s, "SS", d["sc_x1bf"][i * 128:(i + 1) * 128, :], hibb[i % 2],
                  f"x1bfst{i}")

        for i in range(NCH):
            if i >= 2:
                S.wait(s, f"ztt{i - 2}")
            S.dma(s, "LD", xfb[i % 2], d["x"][i * 128:(i + 1) * 128, :], f"xb{i}")
            if i >= 1:
                em_bstores(i - 1)
        em_bstores(NCH - 1)
        if stop < 6:
            return

        def em_x1q(q):
            if q >= 1:
                S.wait(s, f"ffnGU{q - 1}")
            S.wait(s, f"x1bfst{(q + 1) * WCH - 1}")
            for k in range(8):
                lbl = None
                if k == 6:
                    lbl = f"x1qa{q}"
                elif k == 7:
                    lbl = f"x1q{q}"
                S.dmaT(s, "LD", sb["x1q"][:, k, :],
                       d["sc_x1bf"][q * W:(q + 1) * W, k * 128:(k + 1) * 128], lbl)

        def em_xl(i):
            if i == 0:
                S.wait(s, f"x1st{NCH - 1}")
                S.wait(s, f"xbf{NCH - 1}")
            else:
                S.wait(s, f"z2tt{i - 1}")
            S.wait(s, f"x1st{i}")
            S.dma(s, "LD", sb["xlb"], d["sc_x1tm"][i * 128:(i + 1) * 128, :],
                  f"xl{i}")

        em_x1q(0)
        for q in range(NQ):
            for e in range(E):
                if e == 0:
                    S.wait(s, f"wrst{q}")
                if not (q == 0 and e == 0):
                    pq, pe = (q, e - 1) if e > 0 else (q - 1, E - 1)
                    S.wait(s, f"wbc{pq}_{pe}")
                S.dma(s, "LD", sb["wrow0"][0:1, 0:W],
                      d["sc_wr"][e, q * W:(q + 1) * W], f"wrow{q}_{e}")
                if q >= 1 and e < WCH:
                    em_xl((q - 1) * WCH + e)
            if q + 1 < NQ:
                em_x1q(q + 1)
        for c in range(WCH):
            em_xl((NQ - 1) * WCH + c)

    def em_scalar(s):
        for i in range(NCH):
            S.wait(s, f"qk{i}")
            if i >= 1:
                S.wait(s, f"qst{i - 1}")
            ii = s.activation(sb["q_sb"], psf[:, 0:1024], AF.Copy)
            S.inc(ii, "AC", f"qev{i}", 1)
        if stop < 3:
            S.wait_total(s, "ST0")
            S.wait_total(s, "ST1")
            return
        S.wait(s, "arowev")
        S.dma(s, "ST", d["sc_row"].rearrange("(a b) -> a b", b=128),
              sb["arow32"], "arwr")
        if stop == 3 and S3 < 3:
            S.wait_total(s, "ST0")
            S.wait_total(s, "ST1")
            return
        S.wait(s, "gnapply")
        ii = s.activation(sb["h4bf"][0:4, 1:N + 1], sb["h4"][0:4, 1:N + 1],
                          AF.Relu, bias=sb["gnb_sb"], scale=sb["gng_sb"])
        S.inc(ii, "AC", "hrelu", 1)
        if stop < 3:
            S.wait_total(s, "ST0")
            S.wait_total(s, "ST1")
            return
        if stop == 3 and S3 < 3:
            pass
        if not (stop == 3 and (S3 < 5 or S5 < 2)):
            _do_ch = True
        else:
            _do_ch = False
        for oc in (range(2) if _do_ch else []):
            S.wait(s, f"ch1_{oc}")
            ii = s.activation(sb["h1ch"][:, oc:oc + 1], psb32[:, 42 + oc:43 + oc],
                              AF.Relu, bias=sb["chb1_sb"][:, oc:oc + 1])
            S.inc(ii, "AC", f"chrelu{oc}", 1)
        if _do_ch:
            S.wait(s, "negb2")
        for oc in (range(8) if _do_ch else []):
            S.wait(s, f"ch2_{oc}")
            ii = s.activation(sb["ch_w"][:, oc:oc + 1], psb32[:, 44 + oc:45 + oc],
                              _EXPF, scale=-1.0, bias=sb["negb2"][:, oc:oc + 1])
            S.inc(ii, "AC", f"chexp{oc}", 1)
        if stop == 3 and S3 < 4:
            S.wait_total(s, "ST0")
            S.wait_total(s, "ST1")
            return
        S.wait(s, "a1ev")
        S.dma(s, "ST", d["sc_row"][:], sb["arow"][0:1, 1:N + 1], "a1wr")
        S.wait(s, "a1tm")
        ii = s.activation(sb["rt4"], sb["a_tm"], _EXPF, scale=-1.0)
        S.inc(ii, "AC", "siga", 1)
        S.wait(s, "sigtm")
        ii = s.activation(sb["rt5"], sb["a_tm"], _EXPF)
        S.inc(ii, "AC", "expa", 1)
        if stop < 4:
            S.wait_total(s, "ST0")
            S.wait_total(s, "ST1")
            return
        for i in range(NCH):
            if SUB < 1:
                continue
            S.wait(s, f"vb{i}")
            ii = s.activation(sb["tn2"][:, 2:3], sb["mv2"][:, 1:2], AF.Ln)
            s.drain()
            ii = s.activation(sb["tn2"][:, 3:4], sb["tn2"][:, 2:3], AF.Exp,
                              scale=-0.5)
            S.inc(ii, "AC", f"rstd{i}", 1)
        if stop < 6:
            S.wait_total(s, "ST0")
            S.wait_total(s, "ST1")
            return
        for q in range(NQ):
            NEQ = WCH * 8
            S.wait(s, f"rlg{q}")
            ii = s.activation(sb["rt2"][:, 0:NEQ], sb["rt1"][:, 0:NEQ], AF.Exp,
                              scale=-1.0)
            S.inc(ii, "AC", f"rsig{q}", 1)
            S.wait(s, f"rdist{q}")
            ii = s.activation(sb["rt2"][:, 0:NEQ], sb["rt1"][:, 0:NEQ], AF.Exp,
                              scale=-10.0)
            S.inc(ii, "AC", f"rexp1{q}", 1)
            ii = s.activation(sb["rt3"][:, 0:NEQ], sb["rt1"][:, 0:NEQ], AF.Exp,
                              scale=-10.0 / 3.0)
            S.inc(ii, "AC", f"rexp2{q}", 1)
            S.wait(s, f"wrev{q}_{WCH - 1}")
            S.dma(s, "ST", d["sc_wr"][:, q * W:(q + 1) * W],
                  sb["w_rows"][0:8, 0:W], f"wrst{q}")
            for e in range(E):
                if q >= 1 and e < WCH:
                    i = (q - 1) * WCH + e
                    S.wait(s, f"out{i}")
                    S.dma(s, "ST", d["out"][i * 128:(i + 1) * 128, :], sb["obb"],
                          f"outst{i}")
                for oi in range(8):
                    S.wait(s, f"g{q}_{e}_{oi}")
                    jg = (q * E + e) * 8 + oi
                    if jg >= 2:
                        pq, pr = divmod(jg - 2, 8)
                        S.wait(s, f"hw{pq // E}_{pq % E}_{pr}")
                    ii = s.activation(silb[oi % 2], psf[:, 0:W], AF.Silu)
                    S.inc(ii, "AC", f"sil{q}_{e}_{oi}", 1)
        for c in range(WCH):
            i = (NQ - 1) * WCH + c
            S.wait(s, f"out{i}")
            S.dma(s, "ST", d["out"][i * 128:(i + 1) * 128, :], sb["obb"],
                  f"outst{i}")
        S.wait_total(s, "ST0")
        S.wait_total(s, "ST1")

    def em_tensor(t):
        def transpose8_bf(dst, src):
            for kb in range(8):
                ii = t.transpose(dst[:, kb * 128:(kb + 1) * 128],
                                 src[:, kb * 128:(kb + 1) * 128], sb["ident_bf"])
            return ii

        S.wait(t, "identbf")
        S.wait(t, "xa0")
        ii = transpose8_bf(psb[:, 0:1024], xtmp[0])
        S.inc(ii, "PE", "tr0", 1)
        for i in range(NCH):
            S.wait(t, f"xfm{i}")
            if i == 0:
                for p4 in range(4):
                    S.wait(t, f"wqk{p4}")
            if i >= 1:
                S.wait(t, f"qev{i - 1}")
                S.wait(t, f"kst{i - 1}")
            for k in range(8):
                lh = sb["x_fm"][:, k, i * 128:(i + 1) * 128]
                t.matmul(psf[:, 0:512], lh, sb["wqk_sb"][:, k, 0:512],
                         start=(k == 0), stop=(k == 7))
                t.matmul(psf[:, 512:1024], lh, sb["wqk_sb"][:, k, 512:1024],
                         start=(k == 0), stop=(k == 7))
                t.matmul(psf[:, 1024:1536], lh, sb["wqk_sb"][:, k, 1024:1536],
                         start=(k == 0), stop=(k == 7))
                ii = t.matmul(psf[:, 1536:2048], lh, sb["wqk_sb"][:, k, 1536:2048],
                              start=(k == 0), stop=(k == 7))
            S.inc(ii, "PE", f"qk{i}", 1)
            if i + 1 < NCH:
                S.wait(t, f"xa{i + 1}")
                if i >= 1:
                    S.wait(t, f"xfm{i - 1}")
                ii = transpose8_bf(psb[:, ((i + 1) % 2) * 1024:((i + 1) % 2 + 1) * 1024],
                                   xtmp[(i + 1) % 2])
                S.inc(ii, "PE", f"tr{i + 1}", 1)
        S.wait(t, "consts")
        S.wait(t, "consts2")
        S.wait(t, f"xfm{NCH - 1}")
        t.matmul(psb32[:, 57:58], sb["ones_r_f"], sb["scal4_sb"][0:1, 0:1],
                 start=True, stop=True)
        ii = t.matmul(psb32[:, 58:59], sb["ones_r_f"], sb["scal4_sb"][0:1, 1:2],
                      start=True, stop=True)
        S.inc(ii, "PE", "ccbc", 1)
        S.wait(t, "consts")
        S.wait(t, "consts2")
        t.matmul(psf[:, 2048:2560], sb["ones_r_f"], sb["projb_row"][0:1, 0:512],
                 start=True, stop=True)
        ii = t.matmul(psf[:, 2560:3072], sb["ones_r_f"], sb["projb_row"][0:1, 512:1024],
                      start=True, stop=True)
        S.inc(ii, "PE", "pjb", 1)
        if stop < 2:
            return
        S.wait(t, "wv")
        S.wait(t, f"qev{NCH - 1}")
        S.wait(t, f"kst{NCH - 1}")
        for idx in range(NV):
            oc, tq = divmod(idx, NVT)
            if idx >= 2:
                S.wait(t, f"vev{idx - 2}")
            buf = idx % 2
            for k in range(8):
                lh = sb["wv_sb"][:, (oc * 8 + k) * 128:(oc * 8 + k) * 128 + 128]
                for h2 in range(VT // 512):
                    ii = t.matmul(psf[:, buf * 1024 + h2 * 512:buf * 1024 + h2 * 512 + 512],
                                  lh, sb["x_fm"][:, k, tq * VT + h2 * 512:tq * VT + h2 * 512 + 512],
                                  start=(k == 0), stop=(k == 7))
            S.inc(ii, "PE", f"v{idx}", 1)
        if stop < 3:
            return
        S.wait(t, "abase")
        ii = t.transpose(psb32[0:NCH, 128:256], sb["a_tm"], sb["ident_f"])
        S.inc(ii, "PE", "atr", 1)
        if stop == 3 and S3 < 2:
            return
        S.wait(t, "arbf")
        S.wait(t, "cbf")
        S.wait(t, "halo")
        S.wait(t, "pjbev")
        for tc in range(NC5):
            cb = 2048 + 512 * (tc % 2)
            if tc >= 2:
                S.wait(t, f"h4ev{tc - 2}")
            for j in range(3):
                ii = t.matmul(psf[0:4, cb:cb + 512],
                              sb["conv1_bf"][0:1, j * 4:(j + 1) * 4],
                              sb["arow_bf"][0:1, tc * 512 + j:tc * 512 + j + 512],
                              start=(j == 0), stop=(j == 2))
            S.inc(ii, "PE", f"c1_{tc}", 1)
        if stop == 3 and S3 < 3:
            return
        S.wait(t, "h4stats")
        S.wait(t, "consts")
        S.wait(t, "consts2")
        t.matmul(psb32[0:2, 52:53], sb["sel42"], sb["gtmp"][0:4, 0:1],
                 start=True, stop=True)
        ii = t.matmul(psb32[0:2, 53:54], sb["sel42"], sb["gtmp"][0:4, 1:2],
                      start=True, stop=True)
        S.inc(ii, "PE", "gnsum", 1)
        S.wait(t, "gnmr2")
        ii = t.matmul(psb32[0:4, 54:56], sb["sel24"], sb["gn_sc"][0:2, 48:50],
                      start=True, stop=True)
        S.inc(ii, "PE", "gnexp", 1)
        S.wait(t, "hrelu")
        for tc in range(NC5):
            cb = 2048 + 512 * (tc % 2)
            if tc >= 2:
                S.wait(t, f"y1ev{tc - 2}")
            for j in range(3):
                ii = t.matmul(psf[0:1, cb:cb + 512],
                              sb["conv2_bf"][0:4, j:j + 1],
                              sb["h4bf"][0:4, tc * 512 + j:tc * 512 + j + 512],
                              start=(j == 0), stop=(j == 2))
            S.inc(ii, "PE", f"c2_{tc}", 1)
        if stop == 3 and (S3 < 5 or S5 < 2):
            _skip_ch = True
        else:
            _skip_ch = False
        if not _skip_ch:
            S.wait(t, "mbf")
        if not _skip_ch:
            S.wait(t, "gnmr4")
            S.wait(t, "chw1")
            for oc in range(2):
                if oc >= 1:
                    S.wait(t, f"chrelu{oc - 1}")
                for k in range(8):
                    ii = t.matmul(psb32[:, 42 + oc:43 + oc],
                                  sb["chw1_sb"][:, (oc * 8 + k) * 128:(oc * 8 + k) * 128 + 128],
                                  sb["m_bf"][:, k:k + 1], start=(k == 0), stop=(k == 7))
                S.inc(ii, "PE", f"ch1_{oc}", 1)
            S.wait(t, "chrelu0")
            S.wait(t, "chrelu1")
            S.wait(t, "chw2")
            for oc in range(8):
                if oc >= 1:
                    S.wait(t, f"chexp{oc - 1}")
                for ih in range(2):
                    ii = t.matmul(psb32[:, 44 + oc:45 + oc],
                                  sb["chw2_sb"][:, (oc * 2 + ih) * 128:(oc * 2 + ih) * 128 + 128],
                                  sb["h1ch"][:, ih:ih + 1], start=(ih == 0), stop=(ih == 1))
                S.inc(ii, "PE", f"ch2_{oc}", 1)
        if stop == 3 and S3 < 4:
            return
        S.wait(t, "a1rd")
        if not _skip_ch:
            S.wait(t, "chexp7")
        ii = t.transpose(psb32[:, 8:8 + NCH], sb["arow32"],
                         sb["ident_f"][0:NCH, 0:NCH])
        S.inc(ii, "PE", "a1tr", 1)
        S.wait(t, "esum")
        ii = t.matmul(psb32[0:1, 40:41], sb["ones_c_f"], sb["rt4"][:, 0:1],
                      start=True, stop=True)
        S.inc(ii, "PE", "ssum", 1)
        S.wait(t, "invs")
        ii = t.matmul(psb32[:, 41:42], sb["ones_r_f"], sb["consts"][0:1, 3:4],
                      start=True, stop=True)
        S.inc(ii, "PE", "sbc", 1)
        if stop == 3 and S3 < 5:
            return
        S.wait(t, "invt")
        S.wait(t, "wcol")
        ii = t.matmul(psb32[:, 56:57], sb["ones_r_f"], sb["consts"][0:1, 2:3],
                      start=True, stop=True)
        S.inc(ii, "PE", "tbc", 1)
        if stop < 4:
            return
        for i in range(NCH):
            buf = i % 2
            if i < 2:
                S.wait(t, f"vev{NV - 1}")
                S.wait(t, "pp7")
            else:
                S.wait(t, f"ztt{i - 2}")
            for k in range(8):
                lh = sb["v_fm"][:, k, i * 128:(i + 1) * 128]
                t.matmul(psf[:, buf * 1024:buf * 1024 + 512], lh,
                         sb["pp_sb"][:, k, 0:512], start=(k == 0), stop=(k == 7))
                ii = t.matmul(psf[:, buf * 1024 + 512:buf * 1024 + 1024], lh,
                              sb["pp_sb"][:, k, 512:1024], start=(k == 0), stop=(k == 7))
            S.inc(ii, "PE", f"proj{i}", 1)
        if stop < 5:
            return

        def em_lnT(q, c):
            i = q * WCH + c
            S.wait(t, f"acc{q}_{E - 1}_7")
            S.wait(t, f"xl{i}")
            if i == 0:
                S.wait(t, "tcolev")
            else:
                S.wait(t, f"z2tt{i - 1}")
            for kb in range(8):
                ii = t.transpose(psb32[:, kb * 128:kb * 128 + 128],
                                 accb[q % 2][:, kb, c * 128:(c + 1) * 128],
                                 sb["ident_f"])
            S.inc(ii, "PE", f"lnT{i}", 1)

        if stop < 6:
            return
        for q in range(NQ):
            S.wait(t, f"x1q{q}")
            S.wait(t, f"x1qa{q}")
            if q == 0:
                S.wait(t, "wrbf")
                S.wait(t, "tcolev")
                S.wait(t, f"ztt{NCH - 1}")
            else:
                S.wait(t, f"sil{q - 1}_{E - 1}_7")
            for c in range(WCH):
                if c >= 1:
                    S.wait(t, f"lgev{q}_{c - 1}")
                for k in range(8):
                    ii = t.matmul(psf[:, 0:8], sb["x1q"][:, k, c * 128:(c + 1) * 128],
                                  sb["wr_bf"][:, k * 8:(k + 1) * 8],
                                  start=(k == 0), stop=(k == 7))
                S.inc(ii, "PE", f"lgq{q}_{c}", 1)
            S.wait(t, f"wroute{q}")
            for c in range(WCH):
                if c >= 1:
                    S.wait(t, f"wrev{q}_{c - 1}")
                ii = t.transpose(psf[0:8, 512:640],
                                 sb["wroute"][:, c * 8:(c + 1) * 8], sb["ident_f"])
                S.inc(ii, "PE", f"wrt{q}_{c}", 1)
            for e in range(E):
                j3 = (q * E + e) * 3
                if q >= 1 and e < WCH:
                    em_lnT(q - 1, e)
                S.wait(t, f"wrow{q}_{e}")
                if q == 0 and e == 0:
                    S.wait(t, f"x1bfst{NCH - 1}")
                else:
                    pq, pe = (q, e - 1) if e > 0 else (q - 1, E - 1)
                    S.wait(t, f"acc{pq}_{pe}_6")
                for h2 in range(W // 512):
                    ii = t.matmul(psf[:, 2048 + h2 * 512:2048 + h2 * 512 + 512],
                                  sb["ones_r_f"],
                                  sb["wrow0"][0:1, h2 * 512:h2 * 512 + 512],
                                  start=True, stop=True)
                S.inc(ii, "PE", f"wbc{q}_{e}", 1)
                for oi in range(8):
                    if oi == 0:
                        S.wait(t, f"slot{j3}")
                        S.wait(t, f"slot{j3 + 1}")
                        S.wait(t, f"x1q{q}")
                        S.wait(t, f"x1qa{q}")
                        if q == 0 and e == 0:
                            S.wait(t, f"ztt{NCH - 1}")
                    jg = (q * E + e) * 8 + oi
                    if jg >= 1:
                        pj = jg - 1
                        S.wait(t, f"sil{pj // (E * 8)}_{(pj // 8) % E}_{pj % 8}")
                    for k in range(8):
                        lh = sl[j3 % 3][:, (oi * 8 + k) * 128:(oi * 8 + k) * 128 + 128]
                        for h2 in range(W // 512):
                            ii = t.matmul(psf[:, h2 * 512:h2 * 512 + 512], lh,
                                          sb["x1q"][:, k, h2 * 512:h2 * 512 + 512],
                                          start=(k == 0), stop=(k == 7))
                    S.inc(ii, "PE", f"g{q}_{e}_{oi}", 1)
                    if oi == 7:
                        S.alias(f"mmdone{j3}", f"g{q}_{e}_7")
                    if jg >= 1:
                        pj = jg - 1
                        S.wait(t, f"upw{pj // (E * 8)}_{(pj // 8) % E}_{pj % 8}")
                    if oi == 0 and not (q == 0 and e == 0):
                        pq2, pe2 = (q, e - 1) if e > 0 else (q - 1, E - 1)
                        S.wait(t, f"acc{pq2}_{pe2}_7")
                    for k in range(8):
                        lh = sl[(j3 + 1) % 3][:, (oi * 8 + k) * 128:(oi * 8 + k) * 128 + 128]
                        for h2 in range(W // 512):
                            ii = t.matmul(psf[:, 1024 + h2 * 512:1024 + h2 * 512 + 512], lh,
                                          sb["x1q"][:, k, h2 * 512:h2 * 512 + 512],
                                          start=(k == 0), stop=(k == 7))
                    S.inc(ii, "PE", f"u{q}_{e}_{oi}", 1)
                    if oi == 7:
                        S.alias(f"mmdone{j3 + 1}", f"u{q}_{e}_7")
                        if e == E - 1:
                            S.alias(f"ffnGU{q}", f"u{q}_{e}_7")
                for dd in range(8):
                    dbase = 2048 if dd % 2 == 0 else 1024
                    if dd == 0:
                        S.wait(t, f"slot{j3 + 2}")
                        S.wait(t, f"hw{q}_{e}_7")
                        S.wait(t, f"wbcev{q}_{e}")
                    elif dd >= 2:
                        S.wait(t, f"acc{q}_{e}_{dd - 2}")
                    for ih in range(8):
                        lh = sl[(j3 + 2) % 3][:, (dd * 8 + ih) * 128:(dd * 8 + ih) * 128 + 128]
                        for h2 in range(W // 512):
                            ii = t.matmul(psf[:, dbase + h2 * 512:dbase + h2 * 512 + 512], lh,
                                          sb["hbuf"][:, ih, h2 * 512:h2 * 512 + 512],
                                          start=(ih == 0), stop=(ih == 7))
                    S.inc(ii, "PE", f"dn{q}_{e}_{dd}", 1)
                    if dd == 7:
                        S.alias(f"mmdone{j3 + 2}", f"dn{q}_{e}_7")
        for c in range(WCH):
            em_lnT(NQ - 1, c)

    def em_vector(v):
        def dr():
            v.drain()

        def newton_rsqrt(dst, src, p, w):
            t1 = sb["tn1"][0:p, 0:w]
            t2 = sb["tn2"][0:p, 0:w]
            v.tensor_scalar(t1.bitcast(I32), src.bitcast(I32), 1, None,
                            ALU.logical_shift_right)
            dr()
            v.tensor_scalar(t1.bitcast(I32), t1.bitcast(I32), -1, MAGIC,
                            ALU.mult, ALU.add)
            dr()
            last = None
            for it in range(3):
                ys = t1 if it == 0 else dst
                v.tensor_tensor(t2, ys, ys, ALU.mult)
                dr()
                v.tensor_tensor(t2, t2, src, ALU.mult)
                dr()
                v.tensor_scalar(t2, t2, -0.5, 1.5, ALU.mult, ALU.add)
                dr()
                last = v.tensor_tensor(dst, ys, t2, ALU.mult)
                dr()
            return last

        v.memset(sb["ones_r_bf"], 1.0)
        v.memset(sb["ones_r_f"], 1.0)
        v.memset(sb["ones_c_f"], 1.0)
        for i in range(NCH):
            S.wait(v, f"tr{i}")
            ii = v.tensor_copy(sb["x_fm"][:, :, i * 128:(i + 1) * 128],
                               psb[:, (i % 2) * 1024:(i % 2 + 1) * 1024]
                               .rearrange("p (a b) -> p a b", b=128))
            S.inc(ii, "DV", f"xfm{i}", 1)
            S.wait(v, f"qk{i}")
            S.wait(v, f"qev{i}")
            v.bn_stats(sb["st6"][:, 0:6], psf[:, 1024:1536])
            v.bn_stats(sb["st6"][:, 6:12], psf[:, 1536:2048])
            ii = v.tensor_tensor(sb["qk_sc"], sb["q_sb"], psf[:, 1024:2048],
                                 ALU.mult)
            S.inc(ii, "DV", f"kst{i}", 1)
            dr()
            v.reduce_sum(sb["sqk"][:, i:i + 1], sb["qk_sc"], axis=AX)
            dr()
            v.bn_aggr(sb["mv2"], sb["st6"])
            dr()
            v.tensor_copy(sb["muk"][:, i:i + 1], sb["mv2"][:, 0:1])
            v.tensor_copy(sb["vark"][:, i:i + 1], sb["mv2"][:, 1:2])
            v.bn_stats(sb["st6"][:, 0:6], sb["q_sb"][:, 0:512])
            v.bn_stats(sb["st6"][:, 6:12], sb["q_sb"][:, 512:1024])
            dr()
            v.bn_aggr(sb["mv2"], sb["st6"])
            dr()
            v.tensor_copy(sb["muq"][:, i:i + 1], sb["mv2"][:, 0:1])
            ii = v.tensor_copy(sb["varq"][:, i:i + 1], sb["mv2"][:, 1:2])
            S.inc(ii, "DV", f"qst{i}", 1)
            dr()
        if stop < 2:
            return
        S.wait(v, "consts")
        S.wait(v, "consts2")
        v.reciprocal(sb["consts"][0:1, 2:3], sb["scal4_sb"][0:1, 2:3])
        ii = S.inc(v.tensor_scalar(sb["negb2"], sb["chb2_sb"], -1.0, None, ALU.mult),
                   "DV", "negb2", 1)
        S.alias("invt", "negb2")
        dr()
        S.wait(v, "ccbc")
        v.tensor_copy(sb["consts"][:, 0:1], psb32[:, 57:58])
        ii = v.tensor_copy(sb["consts"][:, 1:2], psb32[:, 58:59])
        S.inc(ii, "DV", "ccev", 1)
        dr()
        S.wait(v, "pjb")
        ii = v.tensor_copy(sb["projb_bc"], psf[:, 2048:3072])
        S.inc(ii, "DV", "pjbev", 1)
        if stop < 3:
            return
        c1c = sb["consts"][:, 0:1]
        c2c = sb["consts"][:, 1:2]
        t1 = sb["rt4"]
        t2 = sb["rt5"]
        K1023 = 1024.0 / 1023.0
        v.tensor_tensor(t1, sb["muq"], sb["muk"], ALU.mult)
        dr()
        v.tensor_scalar(t2, sb["sqk"], 1.0 / 1024.0, None, ALU.mult)
        dr()
        v.tensor_tensor(t2, t2, t1, ALU.subtract)
        dr()
        v.tensor_scalar(t2, t2, 2.0 * K1023, c2c, ALU.mult, ALU.add)
        dr()
        v.tensor_scalar(t1, t1, 2.0, c1c, ALU.mult, ALU.add)
        dr()
        v.tensor_tensor(t1, t1, t2, ALU.mult)
        dr()
        v.tensor_tensor(t2, sb["varq"], sb["vark"], ALU.add)
        dr()
        v.tensor_scalar(t2, t2, K1023, c2c, ALU.mult, ALU.add)
        dr()
        v.tensor_tensor(sb["a_tm"], sb["muq"], sb["muq"], ALU.mult)
        dr()
        v.tensor_tensor(sb["qk_sc"][:, 0:NCH], sb["muk"], sb["muk"], ALU.mult)
        dr()
        v.tensor_tensor(sb["a_tm"], sb["a_tm"], sb["qk_sc"][:, 0:NCH], ALU.add)
        dr()
        v.tensor_scalar(sb["a_tm"], sb["a_tm"], c1c, None, ALU.add)
        dr()
        v.tensor_tensor(t2, sb["a_tm"], t2, ALU.mult)
        dr()
        v.tensor_scalar(t2, t2, 1e-7, None, ALU.add)
        dr()
        v.reciprocal(sb["a_tm"], t2)
        dr()
        v.tensor_tensor(t1, t1, sb["a_tm"], ALU.mult)
        dr()
        ii = v.tensor_tensor(sb["a_tm"], t1, t1, ALU.mult)
        S.inc(ii, "DV", "abase", 1)
        if stop == 3 and S3 < 2:
            return
        for idx in range(NV):
            oc, tq = divmod(idx, NVT)
            S.wait(v, f"v{idx}")
            buf = idx % 2
            ii = v.tensor_copy(sb["v_fm"][:, oc, tq * VT:(tq + 1) * VT],
                               psf[:, buf * 1024:buf * 1024 + VT])
            S.inc(ii, "DV", f"vev{idx}", 1)
        dr()
        v.reduce_sum(sb["m_fm"].rearrange("p (a b) -> p a b", b=1), sb["x_fm"],
                     axis=AX)
        dr()
        ii = v.tensor_scalar(sb["m_fm"], sb["m_fm"], 1.0 / N, None, ALU.mult)
        S.inc(ii, "DV", "mfm", 1)
        dr()
        ii = v.tensor_copy(sb["m_bf"], sb["m_fm"])
        S.inc(ii, "DV", "mbf", 1)
        S.wait(v, "atr")
        ii = v.tensor_copy(sb["arow32"], psb32[0:NCH, 128:256])
        S.inc(ii, "DV", "arowev", 1)
        S.wait(v, "consts")
        v.tensor_copy(sb["conv1_bf"], sb["conv1_sb"])
        ii = v.tensor_copy(sb["conv2_bf"], sb["conv2_sb"])
        S.inc(ii, "DV", "cbf", 1)
        S.wait(v, f"tr{NCH - 1}")
        S.wait(v, f"qk{NCH - 1}")
        v.memset(sb["arow"][0:1, 0:1], 0.0)
        v.memset(sb["arow"][0:1, N + 1:N + 2], 0.0)
        v.memset(sb["arow_bf"][0:1, 0:1], 0.0)
        v.memset(sb["arow_bf"][0:1, N + 1:N + 2], 0.0)
        v.memset(sb["h4"][0:4, 0:1], 0.0)
        v.memset(sb["h4"][0:4, N + 1:N + 2], 0.0)
        v.memset(sb["h4bf"][0:4, 0:1], 0.0)
        ii = v.memset(sb["h4bf"][0:4, N + 1:N + 2], 0.0)
        S.inc(ii, "DV", "halo", 1)
        for tc in range(NC5):
            cb = 2048 + 512 * (tc % 2)
            S.wait(v, f"c1_{tc}")
            ii = v.tensor_copy(sb["h4"][0:4, 1 + tc * 512:1 + tc * 512 + 512],
                               psf[0:4, cb:cb + 512])
            S.inc(ii, "DV", f"h4ev{tc}", 1)
        dr()
        if stop == 3 and S3 < 3:
            return
        for b8 in range(NC5):
            v.bn_stats(sb["gn_sc"][0:4, b8 * 6:b8 * 6 + 6],
                       sb["h4"][0:4, 1 + b8 * 512:1 + b8 * 512 + 512])
        dr()
        v.bn_aggr(sb["gn_sc"][0:4, 48:50], sb["gn_sc"][0:4, 0:6 * NC5])
        dr()
        v.tensor_copy(sb["gtmp"][0:4, 0:1], sb["gn_sc"][0:4, 48:49])
        v.tensor_tensor(sb["gtmp"][0:4, 2:3], sb["gn_sc"][0:4, 48:49],
                        sb["gn_sc"][0:4, 48:49], ALU.mult)
        dr()
        ii = v.tensor_tensor(sb["gtmp"][0:4, 1:2], sb["gn_sc"][0:4, 49:50],
                             sb["gtmp"][0:4, 2:3], ALU.add)
        S.inc(ii, "DV", "h4stats", 1)
        S.wait(v, "gnsum")
        v.tensor_scalar(sb["gn_sc"][0:2, 48:49], psb32[0:2, 52:53], 0.5, None,
                        ALU.mult)
        v.tensor_scalar(sb["gn_sc"][0:2, 50:51], psb32[0:2, 53:54], 0.5, None,
                        ALU.mult)
        dr()
        v.tensor_tensor(sb["gn_sc"][0:2, 51:52], sb["gn_sc"][0:2, 48:49],
                        sb["gn_sc"][0:2, 48:49], ALU.mult)
        dr()
        v.tensor_tensor(sb["gn_sc"][0:2, 50:51], sb["gn_sc"][0:2, 50:51],
                        sb["gn_sc"][0:2, 51:52], ALU.subtract)
        dr()
        v.tensor_scalar(sb["gn_sc"][0:2, 50:51], sb["gn_sc"][0:2, 50:51], GN_EPS,
                        None, ALU.add)
        dr()
        ii = newton_rsqrt(sb["gn_sc"][0:2, 49:50], sb["gn_sc"][0:2, 50:51], 2, 1)
        S.inc(ii, "DV", "gnmr2", 1)
        S.wait(v, "gnexp")
        ii = v.tensor_copy(sb["gtmp"][0:4, 4:6], psb32[0:4, 54:56])
        S.inc(ii, "DV", "gnmr4", 1)
        dr()
        ii = v.tensor_scalar(sb["h4"][0:4, 1:N + 1], sb["h4"][0:4, 1:N + 1],
                             sb["gtmp"][0:4, 4:5], sb["gtmp"][0:4, 5:6],
                             ALU.subtract, ALU.mult)
        S.inc(ii, "DV", "gnapply", 1)
        dr()
        for tc in range(NC5):
            cb = 2048 + 512 * (tc % 2)
            S.wait(v, f"c2_{tc}")
            if tc == 0:
                S.wait(v, "arrd")
            v.tensor_scalar(sb["zbuf"][0:1, 0:512], psf[0:1, cb:cb + 512],
                            sb["scal4_sb"][0:1, 3:4], None, ALU.add)
            dr()
            ii = v.tensor_tensor(sb["arow"][0:1, 1 + tc * 512:1 + tc * 512 + 512],
                                 sb["arow"][0:1, 1 + tc * 512:1 + tc * 512 + 512],
                                 sb["zbuf"][0:1, 0:512], ALU.add)
            S.inc(ii, "DV", f"y1ev{tc}", 1)
            dr()
        S.alias("a1ev", f"y1ev{NC5 - 1}")
        if stop == 3 and S3 < 4:
            return
        S.wait(v, "a1tr")
        ii = v.tensor_copy(sb["a_tm"], psb32[:, 8:8 + NCH])
        S.inc(ii, "DV", "a1tm", 1)
        S.wait(v, "siga")
        v.tensor_scalar(sb["rt4"], sb["rt4"], 1.0, None, ALU.add)
        dr()
        ii = v.reciprocal(sb["a_tm"], sb["rt4"])
        S.inc(ii, "DV", "sigtm", 1)
        S.wait(v, "expa")
        ii = v.reduce_sum(sb["rt4"][:, 0:1], sb["rt5"], axis=AX)
        S.inc(ii, "DV", "esum", 1)
        S.wait(v, "ssum")
        ii = v.reciprocal(sb["consts"][0:1, 3:4], psb32[0:1, 40:41])
        S.inc(ii, "DV", "invs", 1)
        S.wait(v, "sbc")
        v.tensor_copy(sb["consts"][:, 3:4], psb32[:, 41:42])
        dr()
        ii = v.tensor_scalar(sb["w_col"], sb["rt5"], sb["consts"][:, 3:4], None,
                             ALU.mult)
        S.inc(ii, "DV", "wcol", 1)
        if stop == 3 and S3 < 5:
            return
        S.wait(v, "tbc")
        ii = v.tensor_copy(sb["consts"][:, 2:3], psb32[:, 56:57])
        S.inc(ii, "DV", "tcolev", 1)
        dr()
        ii = v.tensor_scalar(sb["wr_bf"], sb["wr_f"], sb["consts"][:, 2:3], None,
                             ALU.mult)
        S.inc(ii, "DV", "wrbf", 1)
        if stop == 3 and S5 < 2:
            return
        for oc in range(8):
            S.wait(v, f"chexp{oc}")
        v.tensor_scalar(sb["ch_w"], sb["ch_w"], 1.0, None, ALU.add)
        dr()
        ii = v.reciprocal(sb["ch_w"], sb["ch_w"])
        S.inc(ii, "DV", "chw", 1)
        dr()
        if stop == 3 and S5 < 3:
            return
        S.wait(v, "pjall")
        for k in range(8):
            ii = v.tensor_scalar(sb["pp_sb"][:, k, :], sb["pstg32"][:, k, :],
                                 sb["ch_w"][:, k:k + 1], None, ALU.mult)
            S.inc(ii, "DV", f"ppk{k}", 1)
            dr()
        S.alias("pp7", "ppk7")
        if stop < 4:
            return
        if stop < 4:
            return
        for i in range(NCH):
            buf = i % 2
            S.wait(v, f"proj{i}")
            S.wait(v, f"xb{i}")
            ii = v.scalar_tensor_tensor(sb["zbuf"], psf[:, buf * 1024:buf * 1024 + 1024],
                                        sb["w_col"][:, i:i + 1], xfb[buf],
                                        ALU.mult, ALU.add)
            S.inc(ii, "DV", f"ztt{i}", 1)
            dr()
            v.tensor_tensor(sb["zbuf"], sb["zbuf"], sb["projb_bc"], ALU.add)
            dr()
            v.bn_stats(sb["st6"][:, 0:6], sb["zbuf"][:, 0:512])
            v.bn_stats(sb["st6"][:, 6:12], sb["zbuf"][:, 512:1024])
            dr()
            v.bn_aggr(sb["mv2"], sb["st6"])
            dr()
            ii = v.tensor_scalar(sb["mv2"][:, 1:2], sb["mv2"][:, 1:2], EPS, None,
                                 ALU.add)
            S.inc(ii, "DV", f"vb{i}", 1)
            dr()
            if SUB < 1:
                continue
            if i >= 2:
                S.wait(v, f"x1st{i - 2}")
            S.wait(v, f"rstd{i}")
            ii = v.tensor_scalar(x1bb[i % 2], sb["zbuf"], sb["mv2"][:, 0:1],
                                 sb["tn2"][:, 3:4], ALU.subtract, ALU.mult)
            S.inc(ii, "DV", f"x1{i}", 1)
            if i >= 2:
                S.wait(v, f"x1bfst{i - 2}")
            dr()
            ii = v.tensor_copy(hibb[i % 2], x1bb[i % 2])
            S.inc(ii, "DV", f"xbf{i}", 1)
            dr()
        def em_lnV(q, c):
            i = q * WCH + c
            S.wait(v, f"lnT{i}")
            S.wait(v, f"xl{i}")
            ii = v.tensor_tensor(sb["zbuf"], sb["xlb"], psb32[:, 0:1024], ALU.add)
            S.inc(ii, "DV", f"z2tt{i}", 1)
            dr()
            v.bn_stats(sb["st6"][:, 0:6], sb["zbuf"][:, 0:512])
            v.bn_stats(sb["st6"][:, 6:12], sb["zbuf"][:, 512:1024])
            dr()
            v.bn_aggr(sb["mv2"], sb["st6"])
            dr()
            v.tensor_scalar(sb["mv2"][:, 1:2], sb["mv2"][:, 1:2], EPS, None,
                            ALU.add)
            dr()
            newton_rsqrt(sb["tn1"][:, 2:3], sb["mv2"][:, 1:2], 128, 1)
            if i >= 1:
                S.wait(v, f"outst{i - 1}")
            ii = v.tensor_scalar(sb["obb"], sb["zbuf"], sb["mv2"][:, 0:1],
                                 sb["tn1"][:, 2:3], ALU.subtract, ALU.mult)
            S.inc(ii, "DV", f"out{i}", 1)
            dr()

        if stop < 6:
            return

        NEQ = WCH * 8

        def g3(ap):
            return ap.rearrange("p (a b) -> p a b", b=8)

        def bc3(ap):
            return ap.rearrange("p (a b) -> p a b", b=1).broadcast_to([128, WCH, 8])

        def em_routeV(q):
            for c in range(WCH):
                S.wait(v, f"lgq{q}_{c}")
                ii = v.tensor_copy(sb["rt1"][:, c * 8:(c + 1) * 8], psf[:, 0:8])
                S.inc(ii, "DV", f"lgev{q}_{c}", 1)
                dr()
            S.alias(f"rlg{q}", f"lgev{q}_{WCH - 1}")
            S.wait(v, f"rsig{q}")
            v.tensor_scalar(sb["rt2"][:, 0:NEQ], sb["rt2"][:, 0:NEQ], 1.0, None,
                            ALU.add)
            dr()
            v.reciprocal(sb["rt1"][:, 0:NEQ], sb["rt2"][:, 0:NEQ])
            dr()
            v.tensor_scalar(sb["rt2"][:, 0:NEQ], sb["rt1"][:, 0:NEQ], G_LOWER,
                            None, ALU.is_ge)
            dr()
            v.tensor_scalar(sb["wroute"][:, 0:NEQ], sb["rt1"][:, 0:NEQ], G_UPPER,
                            None, ALU.is_le)
            dr()
            v.tensor_tensor(sb["wroute"][:, 0:NEQ], sb["wroute"][:, 0:NEQ],
                            sb["rt2"][:, 0:NEQ], ALU.mult)
            dr()
            v.tensor_scalar(sb["rt1"][:, 0:NEQ], sb["rt1"][:, 0:NEQ], G_CENTER,
                            None, ALU.subtract)
            dr()
            if q == 0:
                S.wait(v, f"x1bfst{NCH - 1}")
            v.tensor_scalar(sb["rt3"][:, 0:NEQ], sb["rt1"][:, 0:NEQ], -1.0, None,
                            ALU.mult)
            dr()
            ii = v.tensor_tensor(sb["rt1"][:, 0:NEQ], sb["rt1"][:, 0:NEQ],
                                 sb["rt3"][:, 0:NEQ], ALU.max)
            S.inc(ii, "DV", f"rdist{q}", 1)
            S.wait(v, f"rexp1{q}")
            S.wait(v, f"rexp2{q}")
            v.tensor_tensor(sb["rt2"][:, 0:NEQ], sb["rt2"][:, 0:NEQ],
                            sb["wroute"][:, 0:NEQ], ALU.mult)
            dr()
            v.reduce_sum(sb["rt4"][:, 0:WCH].rearrange("p (a b) -> p a b", b=1),
                         g3(sb["rt2"][:, 0:NEQ]), axis=AX)
            dr()
            v.tensor_scalar(sb["rt5"][:, 0:WCH], sb["rt4"][:, 0:WCH], 1e-8, None,
                            ALU.is_lt)
            dr()
            v.reduce_max(sb["rt4"][:, 0:WCH].rearrange("p (a b) -> p a b", b=1),
                         g3(sb["rt3"][:, 0:NEQ]), axis=AX)
            dr()
            v.tensor_tensor(g3(sb["rt1"][:, 0:NEQ]), g3(sb["rt3"][:, 0:NEQ]),
                            bc3(sb["rt4"][:, 0:WCH]), ALU.is_ge)
            dr()
            v.tensor_scalar(sb["rt1"][:, 0:NEQ], sb["rt1"][:, 0:NEQ], -1e30, None,
                            ALU.mult)
            dr()
            v.tensor_tensor(sb["rt1"][:, 0:NEQ], sb["rt1"][:, 0:NEQ],
                            sb["rt3"][:, 0:NEQ], ALU.add)
            dr()
            v.reduce_max(sb["rt4"][:, 0:WCH].rearrange("p (a b) -> p a b", b=1),
                         g3(sb["rt1"][:, 0:NEQ]), axis=AX)
            dr()
            v.tensor_tensor(g3(sb["rt1"][:, 0:NEQ]), g3(sb["rt3"][:, 0:NEQ]),
                            bc3(sb["rt4"][:, 0:WCH]), ALU.is_ge)
            dr()
            v.tensor_tensor(sb["rt3"][:, 0:NEQ], sb["rt3"][:, 0:NEQ],
                            sb["rt1"][:, 0:NEQ], ALU.mult)
            dr()
            v.reduce_sum(sb["rt4"][:, 0:WCH].rearrange("p (a b) -> p a b", b=1),
                         g3(sb["rt3"][:, 0:NEQ]), axis=AX)
            dr()
            v.tensor_scalar(sb["rt4"][:, 0:WCH], sb["rt4"][:, 0:WCH], 1e-8, None,
                            ALU.max)
            dr()
            v.reciprocal(sb["rt4"][:, 0:WCH], sb["rt4"][:, 0:WCH])
            dr()
            v.tensor_tensor(g3(sb["rt3"][:, 0:NEQ]), g3(sb["rt3"][:, 0:NEQ]),
                            bc3(sb["rt4"][:, 0:WCH]), ALU.mult)
            dr()
            v.tensor_tensor(sb["rt3"][:, 0:NEQ], sb["rt3"][:, 0:NEQ],
                            sb["rt2"][:, 0:NEQ], ALU.subtract)
            dr()
            v.tensor_tensor(g3(sb["rt3"][:, 0:NEQ]), g3(sb["rt3"][:, 0:NEQ]),
                            bc3(sb["rt5"][:, 0:WCH]), ALU.mult)
            dr()
            v.tensor_tensor(sb["rt2"][:, 0:NEQ], sb["rt2"][:, 0:NEQ],
                            sb["rt3"][:, 0:NEQ], ALU.add)
            dr()
            v.reduce_sum(sb["rt4"][:, 0:WCH].rearrange("p (a b) -> p a b", b=1),
                         g3(sb["rt2"][:, 0:NEQ]), axis=AX)
            dr()
            v.tensor_scalar(sb["rt4"][:, 0:WCH], sb["rt4"][:, 0:WCH], 1e-8, None,
                            ALU.max)
            dr()
            v.reciprocal(sb["rt4"][:, 0:WCH], sb["rt4"][:, 0:WCH])
            dr()
            ii = v.tensor_tensor(g3(sb["wroute"][:, 0:NEQ]),
                                 g3(sb["rt2"][:, 0:NEQ]),
                                 bc3(sb["rt4"][:, 0:WCH]), ALU.mult)
            S.inc(ii, "DV", f"wroute{q}", 1)
            dr()
            for c in range(WCH):
                S.wait(v, f"wrt{q}_{c}")
                if c == 0 and q >= 1:
                    S.wait(v, f"wrst{q - 1}")
                ii = v.tensor_copy(sb["w_rows"][0:8, c * 128:(c + 1) * 128],
                                   psf[0:8, 512:640])
                S.inc(ii, "DV", f"wrev{q}_{c}", 1)
                dr()

        for q in range(NQ):
            em_routeV(q)
            for e in range(E):
                if q >= 1 and e < WCH:
                    em_lnV(q - 1, e)
                S.wait(v, f"wbc{q}_{e}")
                ii = v.tensor_copy(sb["wbc"], psf[:, 2048:2048 + W])
                S.inc(ii, "DV", f"wbcev{q}_{e}", 1)
                dr()
                for oi in range(8):
                    S.wait(v, f"u{q}_{e}_{oi}")
                    ii = v.tensor_tensor(upwb[oi % 2], psf[:, 1024:1024 + W],
                                         sb["wbc"], ALU.mult)
                    S.inc(ii, "DV", f"upw{q}_{e}_{oi}", 1)
                    dr()
                    S.wait(v, f"sil{q}_{e}_{oi}")
                    if oi == 0 and not (q == 0 and e == 0):
                        pq, pe = (q, e - 1) if e > 0 else (q - 1, E - 1)
                        S.wait(v, f"dn{pq}_{pe}_7")
                    ii = v.tensor_tensor(sb["hbuf"][:, oi, :], upwb[oi % 2],
                                         silb[oi % 2], ALU.mult)
                    S.inc(ii, "DV", f"hw{q}_{e}_{oi}", 1)
                for dd in range(8):
                    dbase = 2048 if dd % 2 == 0 else 1024
                    S.wait(v, f"dn{q}_{e}_{dd}")
                    if e == 0:
                        if q >= 2 and dd == 0:
                            S.wait(v, f"lnT{(q - 2) * WCH + WCH - 1}")
                        ii = v.tensor_copy(accb[q % 2][:, dd, :],
                                           psf[:, dbase:dbase + W])
                    else:
                        ii = v.tensor_tensor(accb[q % 2][:, dd, :],
                                             accb[q % 2][:, dd, :],
                                             psf[:, dbase:dbase + W], ALU.add)
                    S.inc(ii, "DV", f"acc{q}_{e}_{dd}", 1)
                    dr()
        for c in range(WCH):
            em_lnV(NQ - 1, c)

    def run_all(gp, sy, sc, te, ve):
        em_gpsimd(gp)
        em_sync(sy)
        em_scalar(sc)
        em_tensor(te)
        em_vector(ve)

    S.dry = True
    px = EngProxy()
    run_all(px, px, px, px, px)
    S.dry = False
    S.reset()
    with nc.Block() as block:
        @block.gpsimd
        def _(g):
            em_gpsimd(g)

        @block.sync
        def _(sy):
            em_sync(sy)

        @block.scalar
        def _(sc):
            em_scalar(sc)

        @block.tensor
        def _(te):
            em_tensor(te)

        @block.vector
        def _(ve):
            em_vector(ve)
    stack.close()
    return nc


def prep_host(inputs):
    c = np.ascontiguousarray
    qkv = np.asarray(inputs["L0_attn_qkv_w"], np.float32)
    h = {}
    h["wqk"] = c(qkv.reshape(8, 128, 3072)[:, :, :2048].transpose(1, 0, 2))
    h["wv"] = c(qkv[:, 2048:3072].reshape(8, 128, 8, 128)
                .transpose(1, 2, 0, 3).reshape(128, 8192))
    h["projw"] = c(np.asarray(inputs["L0_attn_proj_w"], np.float32)
                   .reshape(8, 128, 1024).transpose(1, 0, 2))
    h["projb"] = c(np.asarray(inputs["L0_attn_proj_b"], np.float32).reshape(1, 1024))
    h["wr"] = c(np.asarray(inputs["L0_ffn_Wr"], np.float32)
                .reshape(8, 128, 8).transpose(1, 0, 2).reshape(128, 64))
    h["chw1"] = c(np.asarray(inputs["L0_attn_ch_w1"], np.float32)
                  .reshape(8, 128, 2, 128).transpose(1, 2, 0, 3).reshape(128, 2048))
    h["chw2"] = c(np.asarray(inputs["L0_attn_ch_w2"], np.float32)
                  .reshape(2, 128, 8, 128).transpose(1, 2, 0, 3).reshape(128, 2048))
    h["chb1"] = c(np.asarray(inputs["L0_attn_ch_b1"], np.float32).reshape(2, 128).T)
    h["chb2"] = c(np.asarray(inputs["L0_attn_ch_b2"], np.float32).reshape(8, 128).T)
    h["conv1"] = c(np.asarray(inputs["L0_attn_conv1_w"], np.float32)[:, 0, :]
                   .T.reshape(1, 12))
    h["conv2"] = c(np.asarray(inputs["L0_attn_conv2_w"], np.float32)[0])
    h["gng"] = c(np.asarray(inputs["L0_attn_gn_g"], np.float32).reshape(4, 1))
    h["gnb"] = c(np.asarray(inputs["L0_attn_gn_b"], np.float32).reshape(4, 1))
    h["scal4"] = np.stack([np.float32(np.asarray(inputs["L0_attn_c1"]).reshape(())),
                           np.float32(np.asarray(inputs["L0_attn_c2"]).reshape(())),
                           np.float32(np.asarray(inputs["L0_ffn_temperature"]).reshape(-1)[0]),
                           np.float32(np.asarray(inputs["L0_attn_conv2_b"]).reshape(-1)[0])]
                          ).reshape(1, 4).astype(np.float32)
    h["eye"] = np.eye(128, dtype=np.float32)
    h["sel42d"] = np.array([[1, 0], [1, 0], [0, 1], [0, 1]], np.float32)
    h["sel24d"] = np.array([[1, 1, 0, 0], [0, 0, 1, 1]], np.float32)
    for nm, key in (("wg", "L0_ffn_Wg"), ("wu", "L0_ffn_Wu"), ("wd", "L0_ffn_Wd")):
        w = np.asarray(inputs[key], np.float32)
        h[nm] = c(w.reshape(E, 8, 128, 8, 128).transpose(0, 2, 3, 1, 4)
                  .reshape(E, 128, 8192))
    return h


def kernel(**inputs):
    x_all = np.ascontiguousarray(np.asarray(inputs["L0_attn_x"], np.float32))
    N = x_all.shape[1]
    h = prep_host(inputs)
    nc = build_program(N)
    in_maps = []
    for b in range(x_all.shape[0]):
        m = dict(h)
        m["x"] = np.ascontiguousarray(x_all[b])
        in_maps.append(m)
    res = run_bass_kernel_spmd(nc, in_maps, core_ids=list(range(len(in_maps))))
    return np.stack([r["out"] for r in res.results], axis=0)


# version 18
# speedup vs baseline: 1.0968x; 1.0968x over previous
import math
import os
import numpy as np
import contextlib
import concourse.bass as bass
import concourse.mybir as mybir
from concourse.bass_utils import run_bass_kernel_spmd

F32 = mybir.dt.float32
BF16 = mybir.dt.bfloat16
I32 = mybir.dt.int32
AF = mybir.ActivationFunctionType
ALU = mybir.AluOpType
AX = mybir.AxisListType.X

CD = 1024
E = 8
EPS = 1e-5
GN_EPS = 1e-5
G_CENTER = 1.0 / math.e
G_LOWER = 0.5 - math.log(4.0 / 3.0)
G_UPPER = 0.5
MAGIC = 0x5F3759DF


class Dummy:
    def then_inc(self, *a, **k):
        return self


class EngProxy:
    def __getattr__(self, name):
        return lambda *a, **k: Dummy()


SEM_NAMES = ["XA0", "XA1", "WL0", "WL1", "LD0", "LD1", "ST0", "ST1",
             "SS0", "SS1", "PE", "DV", "AC", "GP"]


class Sched:
    def __init__(self):
        self.dry = True
        self.lab = {}
        self.counts = {}
        self.sems = {}
        self.dcount = {}

    def reset(self):
        self.counts = {k: 0 for k in SEM_NAMES}
        self.dcount = {k: 0 for k in ["XA", "WL", "LD", "ST", "SS"]}

    def dma(self, eng, stream, out, in_, label=None):
        idx = self.dcount[stream]
        self.dcount[stream] += 1
        snm = stream + str(idx % 2)
        prev = self.counts[snm]
        if not self.dry and prev > 0:
            eng.wait_ge(self.sems[snm], prev)
        ii = eng.dma_start(out=out, in_=in_)
        return self.inc(ii, snm, label, 16)

    def dmaT(self, eng, stream, out, in_, label=None):
        idx = self.dcount[stream]
        self.dcount[stream] += 1
        snm = stream + str(idx % 2)
        prev = self.counts[snm]
        if not self.dry and prev > 0:
            eng.wait_ge(self.sems[snm], prev)
        ii = eng.dma_start_transpose(out, in_)
        return self.inc(ii, snm, label, 16)

    def inc(self, instr, sem, label=None, n=1):
        self.counts[sem] += n
        if self.dry:
            if label is not None:
                assert label not in self.lab, f"dup label {label}"
                self.lab[label] = (sem, self.counts[sem])
        else:
            instr.then_inc(self.sems[sem], n)
        return instr

    def alias(self, new, old):
        if self.dry:
            self.lab[new] = self.lab[old]

    def wait(self, eng, label):
        if self.dry:
            return
        sem, v = self.lab[label]
        eng.wait_ge(self.sems[sem], v)

    def wait_total(self, eng, sem):
        if not self.dry:
            eng.wait_ge(self.sems[sem], self.counts[sem])


def build_program(N, stop=6):
    SUB = int(os.environ.get('KSUB', '4'))
    _EXPF = AF.Relu if os.environ.get('KEXP') == '0' else AF.Exp
    S3 = int(os.environ.get('KS3', '5'))
    S5 = int(os.environ.get('KS5', '3'))
    NCH = N // 128
    W = min(1024, N)
    NQ = N // W
    WCH = W // 128
    VT = min(1024, N)
    NVT = N // VT
    NV = 8 * NVT
    NC5 = N // 512

    nc = bass.Bass("TRN2", target_bir_lowering=False)

    d = {}
    d["x"] = nc.dram_tensor("x", [N, CD], F32, kind="ExternalInput").ap()
    d["wqk"] = nc.dram_tensor("wqk", [128, 8, 2048], F32, kind="ExternalInput").ap()
    d["wv"] = nc.dram_tensor("wv", [128, 8192], F32, kind="ExternalInput").ap()
    d["projw"] = nc.dram_tensor("projw", [128, 8, 1024], F32, kind="ExternalInput").ap()
    d["projb"] = nc.dram_tensor("projb", [1, 1024], F32, kind="ExternalInput").ap()
    d["wr"] = nc.dram_tensor("wr", [128, 64], F32, kind="ExternalInput").ap()
    d["chw1"] = nc.dram_tensor("chw1", [128, 2048], F32, kind="ExternalInput").ap()
    d["chw2"] = nc.dram_tensor("chw2", [128, 2048], F32, kind="ExternalInput").ap()
    d["chb1"] = nc.dram_tensor("chb1", [128, 2], F32, kind="ExternalInput").ap()
    d["chb2"] = nc.dram_tensor("chb2", [128, 8], F32, kind="ExternalInput").ap()
    d["conv1"] = nc.dram_tensor("conv1", [1, 12], F32, kind="ExternalInput").ap()
    d["conv2"] = nc.dram_tensor("conv2", [4, 3], F32, kind="ExternalInput").ap()
    d["gng"] = nc.dram_tensor("gng", [4, 1], F32, kind="ExternalInput").ap()
    d["gnb"] = nc.dram_tensor("gnb", [4, 1], F32, kind="ExternalInput").ap()
    d["scal4"] = nc.dram_tensor("scal4", [1, 4], F32, kind="ExternalInput").ap()
    d["eye"] = nc.dram_tensor("eye", [128, 128], F32, kind="ExternalInput").ap()
    d["sel42d"] = nc.dram_tensor("sel42d", [4, 2], F32, kind="ExternalInput").ap()
    d["sel24d"] = nc.dram_tensor("sel24d", [2, 4], F32, kind="ExternalInput").ap()
    for nm in ("wg", "wu", "wd"):
        d[nm] = nc.dram_tensor(nm, [E, 128, 8192], F32, kind="ExternalInput").ap()
    d["out"] = nc.dram_tensor("out", [N, CD], F32, kind="ExternalOutput").ap()
    d["sc_x1tm"] = nc.dram_tensor("sc_x1tm", [N, CD], F32, kind="Internal").ap()
    d["sc_x1bf"] = nc.dram_tensor("sc_x1bf", [N, CD], BF16, kind="Internal").ap()
    d["sc_row"] = nc.dram_tensor("sc_row", [N], F32, kind="Internal").ap()
    d["sc_wr"] = nc.dram_tensor("sc_wr", [8, N], F32, kind="Internal").ap()

    TOP = nc.sbuf_top
    base = ((nc.sbuf_base + 511) // 512) * 512
    sb = {}

    class Bump:
        def __init__(self, start):
            self.o = start

        def take(self, nbytes):
            r = self.o
            self.o += ((nbytes + 31) // 32) * 32
            return r

    bump = Bump(base)

    def alloc(name, shape, dtype, at=None):
        nbytes = int(np.prod(shape[1:])) * (2 if dtype == BF16 else 4)
        o = bump.take(nbytes) if at is None else at
        t = nc.alloc_sbuf_tensor_at(name, list(shape), dtype, offset=o,
                                    align_bytes=32)
        sb[name] = t.ap()
        return sb[name]

    alloc("ident_bf", [128, 128], BF16)
    alloc("ident_f", [128, 128], F32)
    alloc("ones_r_bf", [1, 128], BF16)
    alloc("ones_r_f", [1, 128], F32)
    alloc("ones_c_f", [128, 1], F32)
    alloc("muq", [128, NCH], F32)
    alloc("varq", [128, NCH], F32)
    alloc("muk", [128, NCH], F32)
    alloc("vark", [128, NCH], F32)
    alloc("sqk", [128, NCH], F32)
    alloc("a_tm", [128, NCH], F32)
    alloc("w_col", [128, NCH], F32)
    alloc("st6", [128, 12], F32)
    alloc("mv2", [128, 2], F32)
    alloc("tn1", [128, 4], F32)
    alloc("tn2", [128, 4], F32)
    alloc("consts", [128, 8], F32)
    alloc("m_fm", [128, 8], F32)
    alloc("m_bf", [128, 8], BF16)
    alloc("h1ch", [128, 2], BF16)
    alloc("ch_w", [128, 8], F32)
    alloc("negb2", [128, 8], F32)
    alloc("chb1_sb", [128, 2], F32)
    alloc("chb2_sb", [128, 8], F32)
    alloc("scal4_sb", [1, 4], F32)
    alloc("conv1_sb", [1, 12], F32)
    alloc("conv2_sb", [4, 3], F32)
    alloc("conv1_bf", [1, 12], BF16)
    alloc("conv2_bf", [4, 3], BF16)
    alloc("gng_sb", [4, 1], F32)
    alloc("gnb_sb", [4, 1], F32)
    alloc("sel42", [4, 2], F32)
    alloc("sel24", [2, 4], F32)
    alloc("gn_sc", [4, 56], F32)
    alloc("gtmp", [4, 8], F32)
    alloc("arow32", [NCH, 128], F32)
    alloc("wr_f", [128, 64], F32)
    alloc("wr_bf", [128, 64], BF16)
    alloc("wroute", [128, NCH * 8], F32)
    alloc("rt4", [128, NCH], F32)
    alloc("rt5", [128, NCH], F32)
    pbbc = bump.take(4096)
    alloc("projb_bc", [128, 1024], F32, at=pbbc)
    alloc("wbc", [128, W], F32, at=pbbc)

    SCR = bump.take(16896)
    alloc("xa0", [128, 1024], BF16, at=SCR)
    alloc("xa1", [128, 1024], BF16, at=SCR + 2048)
    alloc("q_sb", [128, 1024], F32, at=SCR + 4096)
    alloc("rt1", [128, NCH * 8], F32, at=SCR + 4096)
    alloc("qk_sc", [128, 1024], F32, at=SCR + 8192)
    alloc("rt3", [128, NCH * 8], F32, at=SCR + 8192)
    alloc("projb_row", [1, 1024], F32, at=SCR + 12288)
    alloc("rt2", [128, NCH * 8], F32, at=SCR + 12288)
    alloc("arow", [1, N + 8], F32, at=SCR)
    alloc("w_rows", [8, N], F32, at=SCR)

    ded = bump.take(4096)
    alloc("zbuf", [128, 1024], F32, at=ded)
    ded2 = bump.take(4096)
    alloc("x1b", [128, 1024], F32, at=ded2)
    alloc("obb", [128, 1024], F32, at=ded2)
    alloc("hib", [128, 1024], BF16, at=SCR + 8192)
    alloc("hib1", [128, 1024], BF16, at=SCR + 10240)
    ded4 = bump.take(4096)
    alloc("xlb", [128, 1024], F32, at=ded4)
    alloc("x1b1", [128, 1024], F32, at=ded4)
    ded5 = bump.take(4096)
    alloc("xfb0", [128, 1024], F32, at=ded5)
    alloc("wrow0", [1, 1024], F32, at=ded5)
    ded6 = bump.take(4096)
    alloc("xfb1", [128, 1024], F32, at=ded6)
    alloc("wrowb0", [1, 1024], BF16, at=ded6)
    alloc("wrowb1", [1, 1024], BF16, at=ded6 + 2048)

    BB = ((bump.o + 511) // 512) * 512
    K = 1024
    if N == 4096:
        alloc("x_fm", [128, 8, N], BF16, at=BB)
        alloc("slot0", [128, 8192], BF16, at=BB)
        alloc("chw1_sb", [128, 2048], BF16, at=BB)
        alloc("chw2_sb", [128, 2048], BF16, at=BB + 4096)
        alloc("slot1", [128, 8192], BF16, at=BB + 16 * K)
        alloc("pp_sb", [128, 8, 1024], BF16, at=BB + 32 * K)
        alloc("slot2", [128, 8192], BF16, at=BB + 32 * K)
        alloc("x1q", [128, 8, W], BF16, at=BB + 48 * K)
        alloc("pstg32", [128, 8, 1024], F32, at=BB + 48 * K)
        alloc("wqk_sb", [128, 8, 2048], BF16, at=BB + 64 * K)
        alloc("wv_sb", [128, 8192], BF16, at=BB + 64 * K)
        alloc("hbuf", [128, 8, W], BF16, at=BB + 64 * K)
        alloc("sil0", [128, W], BF16, at=BB + 80 * K)
        alloc("sil1", [128, W], BF16, at=BB + 80 * K + 2 * W)
        alloc("upw0", [128, W], BF16, at=BB + 80 * K + 4 * W)
        alloc("upw1", [128, W], BF16, at=BB + 80 * K + 6 * W)
        alloc("v_fm", [128, 8, N], BF16, at=BB + 80 * K)
        alloc("acc0", [128, 8, W], F32, at=BB + 88 * K)
        alloc("acc1", [128, 8, W], F32, at=BB + 88 * K + 32 * W)
        alloc("h4", [4, N + 8], F32, at=BB + 80 * K + 16 * N)
        alloc("arow_bf", [1, N + 2], BF16, at=BB + 88 * K)
        alloc("h4bf", [4, N + 2], BF16, at=BB + 88 * K + 8256)
        top_need = max(BB + 80 * K + 16 * N + 4 * (N + 8),
                       BB + 88 * K + 64 * W)
    else:
        o = BB
        alloc("x_fm", [128, 8, N], BF16, at=o); o += 16 * N
        wqk_at = o
        alloc("wqk_sb", [128, 8, 2048], BF16, at=o); o += 32 * K
        alloc("h4", [4, N + 8], F32, at=wqk_at)
        alloc("h4bf", [4, N + 2], BF16, at=wqk_at + 12 * K)
        alloc("arow_bf", [1, N + 2], BF16, at=wqk_at + 18 * K)
        alloc("sil0", [128, W], BF16, at=wqk_at + 4 * K)
        alloc("sil1", [128, W], BF16, at=wqk_at + 4 * K + 2 * W)
        alloc("upw0", [128, W], BF16, at=wqk_at + 4 * K + 4 * W)
        alloc("upw1", [128, W], BF16, at=wqk_at + 4 * K + 6 * W)
        alloc("wv_sb", [128, 8192], BF16, at=o)
        alloc("hbuf", [128, 8, W], BF16, at=o); o += 16 * K
        alloc("v_fm", [128, 8, N], BF16, at=o); o += 16 * N
        alloc("slot0", [128, 8192], BF16, at=o)
        alloc("chw1_sb", [128, 2048], BF16, at=o)
        alloc("chw2_sb", [128, 2048], BF16, at=o + 4096)
        o += 16 * K
        alloc("slot1", [128, 8192], BF16, at=o); o += 16 * K
        alloc("pp_sb", [128, 8, 1024], BF16, at=o)
        alloc("slot2", [128, 8192], BF16, at=o); o += 16 * K
        alloc("x1q", [128, 8, W], BF16, at=o); o += 2 * 8 * W
        alloc("pstg32", [128, 8, 1024], F32, at=o)
        alloc("acc0", [128, 8, W], F32, at=o); o += 32 * W
        alloc("acc1", [128, 8, W], F32, at=o); o += 32 * W
        top_need = max(o, o - 64 * W + 32 * 1024)
    assert top_need <= TOP, f"SBUF overflow: need {top_need}, top {TOP}"

    stack = contextlib.ExitStack()
    PSF = stack.enter_context(nc.psum_tensor("PSF", [128, 3072], F32))
    PSB = stack.enter_context(nc.psum_tensor("PSB", [128, 2048], BF16))
    psf = PSF[:, :]
    psb = PSB[:, :]
    psb32 = PSB[:, :].bitcast(F32)

    S = Sched()
    for snm in SEM_NAMES:
        S.sems[snm] = stack.enter_context(nc.semaphore(snm))
    S.reset()

    sl = [sb["slot0"], sb["slot1"], sb["slot2"]]
    silb = [sb["sil0"], sb["sil1"]]
    upwb = [sb["upw0"], sb["upw1"]]
    accb = [sb["acc0"], sb["acc1"]]
    xtmp = [sb["xa0"], sb["xa1"]]
    xfb = [sb["xfb0"], sb["xfb1"]]
    x1bb = [sb["x1b"], sb["x1b1"]]
    hibb = [sb["hib"], sb["hib1"]]


    def em_gpsimd(g):
        S.dma(g, "WL", sb["ident_bf"], d["eye"], "identbf")
        for p4 in range(4):
            S.dma(g, "WL", sb["wqk_sb"][:, 2 * p4:2 * p4 + 2, :],
                  d["wqk"][:, 2 * p4:2 * p4 + 2, :], f"wqk{p4}")
        for i in range(NCH):
            if i >= 2:
                S.wait(g, f"tr{i - 2}")
            S.dma(g, "XA", xtmp[i % 2], d["x"][i * 128:(i + 1) * 128, :], f"xa{i}")
        if stop < 2:
            return
        S.wait(g, f"qk{NCH - 1}")
        S.dma(g, "WL", sb["wv_sb"], d["wv"], "wv")
        if stop < 3 or (stop == 3 and S3 < 2):
            return
        S.wait(g, "arwr")
        S.wait(g, f"v{NV - 1}")
        S.dma(g, "WL", sb["arow_bf"][0:1, 1:N + 1], d["sc_row"][:], "arbf")
        if stop == 3 and (S3 < 5 or S5 < 2):
            return
        S.wait(g, f"v{NV - 1}")
        S.wait(g, "mfm")
        S.dma(g, "WL", sb["chw1_sb"], d["chw1"], "chw1")
        S.dma(g, "WL", sb["chw2_sb"], d["chw2"], "chw2")
        if stop < 6:
            return
        wsrc = {0: d["wg"], 1: d["wu"], 2: d["wd"]}
        for q in range(NQ):
            for e in range(E):
                for m in range(3):
                    j = (q * E + e) * 3 + m
                    if j == 0:
                        S.wait(g, "ch2_7")
                    if j == 2:
                        S.wait(g, f"proj{NCH - 1}")
                    if j >= 3:
                        S.wait(g, f"mmdone{j - 3}")
                    S.dma(g, "WL", sl[j % 3], wsrc[m][e], f"slot{j}")

    def em_sync(s):
        S.dma(s, "LD", sb["ident_f"], d["eye"])
        for src, dst in [(d["sel42d"], sb["sel42"]), (d["sel24d"], sb["sel24"]),
                         (d["scal4"], sb["scal4_sb"]), (d["conv1"], sb["conv1_sb"]),
                         (d["conv2"], sb["conv2_sb"]), (d["gng"], sb["gng_sb"]),
                         (d["gnb"], sb["gnb_sb"]), (d["chb1"], sb["chb1_sb"]),
                         (d["chb2"], sb["chb2_sb"])]:
            S.dma(s, "LD", dst, src)
        S.dma(s, "LD", sb["wr_f"], d["wr"], "consts2")
        S.dma(s, "LD", sb["projb_row"], d["projb"], "consts")
        if stop < 3:
            return
        if stop == 3 and S3 < 2:
            return
        S.wait(s, "arwr")
        S.dma(s, "LD", sb["arow"][0:1, 1:N + 1], d["sc_row"][:], "arrd")
        if stop == 3 and S3 < 4:
            return
        S.wait(s, "a1wr")
        S.dma(s, "LD", sb["arow32"],
              d["sc_row"].rearrange("(a b) -> a b", b=128), "a1rd")
        if stop == 3 and (S3 < 5 or S5 < 3):
            return
        S.wait(s, f"v{NV - 1}")
        S.dma(s, "LD", sb["pstg32"], d["projw"], "pjall")
        if stop < 4:
            return
        def em_bstores(i):
            S.wait(s, f"x1{i}")
            S.dma(s, "SS", d["sc_x1tm"][i * 128:(i + 1) * 128, :], x1bb[i % 2],
                  f"x1st{i}")
            S.wait(s, f"xbf{i}")
            S.dma(s, "SS", d["sc_x1bf"][i * 128:(i + 1) * 128, :], hibb[i % 2],
                  f"x1bfst{i}")

        for i in range(NCH):
            if i >= 2:
                S.wait(s, f"ztt{i - 2}")
            S.dma(s, "LD", xfb[i % 2], d["x"][i * 128:(i + 1) * 128, :], f"xb{i}")
            if i >= 1:
                em_bstores(i - 1)
        em_bstores(NCH - 1)
        if stop < 6:
            return

        def em_x1q(q):
            if q >= 1:
                S.wait(s, f"ffnGU{q - 1}")
            S.wait(s, f"x1bfst{(q + 1) * WCH - 1}")
            for k in range(8):
                lbl = None
                if k == 6:
                    lbl = f"x1qa{q}"
                elif k == 7:
                    lbl = f"x1q{q}"
                S.dmaT(s, "LD", sb["x1q"][:, k, :],
                       d["sc_x1bf"][q * W:(q + 1) * W, k * 128:(k + 1) * 128], lbl)

        def em_xl(i):
            if i == 0:
                S.wait(s, f"x1st{NCH - 1}")
                S.wait(s, f"xbf{NCH - 1}")
            else:
                S.wait(s, f"z2tt{i - 1}")
            S.wait(s, f"x1st{i}")
            S.dma(s, "LD", sb["xlb"], d["sc_x1tm"][i * 128:(i + 1) * 128, :],
                  f"xl{i}")

        em_x1q(0)
        for q in range(NQ):
            for e in range(E):
                if e == 0:
                    S.wait(s, f"wrst{q}")
                if not (q == 0 and e == 0):
                    pq, pe = (q, e - 1) if e > 0 else (q - 1, E - 1)
                    S.wait(s, f"wbc{pq}_{pe}")
                S.dma(s, "LD", sb["wrow0"][0:1, 0:W],
                      d["sc_wr"][e, q * W:(q + 1) * W], f"wrow{q}_{e}")
                if q >= 1 and e < WCH:
                    em_xl((q - 1) * WCH + e)
            if q + 1 < NQ:
                em_x1q(q + 1)
        for c in range(WCH):
            em_xl((NQ - 1) * WCH + c)

    def em_scalar(s):
        def em_xfm(i):
            S.wait(s, f"tr{i}")
            ii = s.activation(sb["x_fm"][:, :, i * 128:(i + 1) * 128],
                              psb[:, (i % 2) * 1024:(i % 2 + 1) * 1024]
                              .rearrange("p (a b) -> p a b", b=128), AF.Copy)
            S.inc(ii, "AC", f"xfm{i}", 1)

        em_xfm(0)
        for i in range(NCH):
            if i + 1 < NCH:
                em_xfm(i + 1)
            S.wait(s, f"qk{i}")
            if i >= 1:
                S.wait(s, f"qst{i - 1}")
            ii = s.activation(sb["q_sb"], psf[:, 0:1024], AF.Copy)
            S.inc(ii, "AC", f"qev{i}", 1)
        if stop < 3:
            S.wait_total(s, "ST0")
            S.wait_total(s, "ST1")
            return
        S.wait(s, "arowev")
        S.dma(s, "ST", d["sc_row"].rearrange("(a b) -> a b", b=128),
              sb["arow32"], "arwr")
        if stop == 3 and S3 < 3:
            S.wait_total(s, "ST0")
            S.wait_total(s, "ST1")
            return
        S.wait(s, "gnapply")
        ii = s.activation(sb["h4bf"][0:4, 1:N + 1], sb["h4"][0:4, 1:N + 1],
                          AF.Relu, bias=sb["gnb_sb"], scale=sb["gng_sb"])
        S.inc(ii, "AC", "hrelu", 1)
        if stop < 3:
            S.wait_total(s, "ST0")
            S.wait_total(s, "ST1")
            return
        if stop == 3 and S3 < 3:
            pass
        if not (stop == 3 and (S3 < 5 or S5 < 2)):
            _do_ch = True
        else:
            _do_ch = False
        for oc in (range(2) if _do_ch else []):
            S.wait(s, f"ch1_{oc}")
            ii = s.activation(sb["h1ch"][:, oc:oc + 1], psb32[:, 42 + oc:43 + oc],
                              AF.Relu, bias=sb["chb1_sb"][:, oc:oc + 1])
            S.inc(ii, "AC", f"chrelu{oc}", 1)
        if _do_ch:
            S.wait(s, "negb2")
        for oc in (range(8) if _do_ch else []):
            S.wait(s, f"ch2_{oc}")
            ii = s.activation(sb["ch_w"][:, oc:oc + 1], psb32[:, 44 + oc:45 + oc],
                              _EXPF, scale=-1.0, bias=sb["negb2"][:, oc:oc + 1])
            S.inc(ii, "AC", f"chexp{oc}", 1)
        if stop == 3 and S3 < 4:
            S.wait_total(s, "ST0")
            S.wait_total(s, "ST1")
            return
        S.wait(s, "a1ev")
        S.dma(s, "ST", d["sc_row"][:], sb["arow"][0:1, 1:N + 1], "a1wr")
        S.wait(s, "a1tm")
        ii = s.activation(sb["rt4"], sb["a_tm"], _EXPF, scale=-1.0)
        S.inc(ii, "AC", "siga", 1)
        S.wait(s, "sigtm")
        ii = s.activation(sb["rt5"], sb["a_tm"], _EXPF)
        S.inc(ii, "AC", "expa", 1)
        if stop < 4:
            S.wait_total(s, "ST0")
            S.wait_total(s, "ST1")
            return
        for i in range(NCH):
            if SUB < 1:
                continue
            S.wait(s, f"vb{i}")
            ii = s.activation(sb["tn2"][:, 2:3], sb["mv2"][:, 1:2], AF.Ln)
            s.drain()
            ii = s.activation(sb["tn2"][:, 3:4], sb["tn2"][:, 2:3], AF.Exp,
                              scale=-0.5)
            S.inc(ii, "AC", f"rstd{i}", 1)
        if stop < 6:
            S.wait_total(s, "ST0")
            S.wait_total(s, "ST1")
            return
        for q in range(NQ):
            NEQ = WCH * 8
            S.wait(s, f"rlg{q}")
            ii = s.activation(sb["rt2"][:, 0:NEQ], sb["rt1"][:, 0:NEQ], AF.Exp,
                              scale=-1.0)
            S.inc(ii, "AC", f"rsig{q}", 1)
            S.wait(s, f"rdist{q}")
            ii = s.activation(sb["rt2"][:, 0:NEQ], sb["rt1"][:, 0:NEQ], AF.Exp,
                              scale=-10.0)
            S.inc(ii, "AC", f"rexp1{q}", 1)
            ii = s.activation(sb["rt3"][:, 0:NEQ], sb["rt1"][:, 0:NEQ], AF.Exp,
                              scale=-10.0 / 3.0)
            S.inc(ii, "AC", f"rexp2{q}", 1)
            S.wait(s, f"wrev{q}_{WCH - 1}")
            S.dma(s, "ST", d["sc_wr"][:, q * W:(q + 1) * W],
                  sb["w_rows"][0:8, 0:W], f"wrst{q}")
            for e in range(E):
                if q >= 1 and e < WCH:
                    i = (q - 1) * WCH + e
                    S.wait(s, f"out{i}")
                    S.dma(s, "ST", d["out"][i * 128:(i + 1) * 128, :], sb["obb"],
                          f"outst{i}")
                for oi in range(8):
                    S.wait(s, f"g{q}_{e}_{oi}")
                    jg = (q * E + e) * 8 + oi
                    if jg >= 2:
                        pq, pr = divmod(jg - 2, 8)
                        S.wait(s, f"hw{pq // E}_{pq % E}_{pr}")
                    ii = s.activation(silb[oi % 2], psf[:, 0:W], AF.Silu)
                    S.inc(ii, "AC", f"sil{q}_{e}_{oi}", 1)
        for c in range(WCH):
            i = (NQ - 1) * WCH + c
            S.wait(s, f"out{i}")
            S.dma(s, "ST", d["out"][i * 128:(i + 1) * 128, :], sb["obb"],
                  f"outst{i}")
        S.wait_total(s, "ST0")
        S.wait_total(s, "ST1")

    def em_tensor(t):
        def transpose8_bf(dst, src):
            for kb in range(8):
                ii = t.transpose(dst[:, kb * 128:(kb + 1) * 128],
                                 src[:, kb * 128:(kb + 1) * 128], sb["ident_bf"])
            return ii

        S.wait(t, "identbf")
        S.wait(t, "xa0")
        ii = transpose8_bf(psb[:, 0:1024], xtmp[0])
        S.inc(ii, "PE", "tr0", 1)
        for i in range(NCH):
            S.wait(t, f"xfm{i}")
            if i == 0:
                for p4 in range(4):
                    S.wait(t, f"wqk{p4}")
            if i >= 1:
                S.wait(t, f"qev{i - 1}")
                S.wait(t, f"kst{i - 1}")
            for k in range(8):
                lh = sb["x_fm"][:, k, i * 128:(i + 1) * 128]
                t.matmul(psf[:, 0:512], lh, sb["wqk_sb"][:, k, 0:512],
                         start=(k == 0), stop=(k == 7))
                t.matmul(psf[:, 512:1024], lh, sb["wqk_sb"][:, k, 512:1024],
                         start=(k == 0), stop=(k == 7))
                t.matmul(psf[:, 1024:1536], lh, sb["wqk_sb"][:, k, 1024:1536],
                         start=(k == 0), stop=(k == 7))
                ii = t.matmul(psf[:, 1536:2048], lh, sb["wqk_sb"][:, k, 1536:2048],
                              start=(k == 0), stop=(k == 7))
            S.inc(ii, "PE", f"qk{i}", 1)
            if i + 1 < NCH:
                S.wait(t, f"xa{i + 1}")
                if i >= 1:
                    S.wait(t, f"xfm{i - 1}")
                ii = transpose8_bf(psb[:, ((i + 1) % 2) * 1024:((i + 1) % 2 + 1) * 1024],
                                   xtmp[(i + 1) % 2])
                S.inc(ii, "PE", f"tr{i + 1}", 1)
        S.wait(t, "consts")
        S.wait(t, "consts2")
        S.wait(t, f"xfm{NCH - 1}")
        t.matmul(psb32[:, 57:58], sb["ones_r_f"], sb["scal4_sb"][0:1, 0:1],
                 start=True, stop=True)
        ii = t.matmul(psb32[:, 58:59], sb["ones_r_f"], sb["scal4_sb"][0:1, 1:2],
                      start=True, stop=True)
        S.inc(ii, "PE", "ccbc", 1)
        S.wait(t, "consts")
        S.wait(t, "consts2")
        t.matmul(psf[:, 2048:2560], sb["ones_r_f"], sb["projb_row"][0:1, 0:512],
                 start=True, stop=True)
        ii = t.matmul(psf[:, 2560:3072], sb["ones_r_f"], sb["projb_row"][0:1, 512:1024],
                      start=True, stop=True)
        S.inc(ii, "PE", "pjb", 1)
        if stop < 2:
            return
        S.wait(t, "wv")
        S.wait(t, f"qev{NCH - 1}")
        S.wait(t, f"kst{NCH - 1}")
        for idx in range(NV):
            oc, tq = divmod(idx, NVT)
            if idx >= 2:
                S.wait(t, f"vev{idx - 2}")
            buf = idx % 2
            for k in range(8):
                lh = sb["wv_sb"][:, (oc * 8 + k) * 128:(oc * 8 + k) * 128 + 128]
                for h2 in range(VT // 512):
                    ii = t.matmul(psf[:, buf * 1024 + h2 * 512:buf * 1024 + h2 * 512 + 512],
                                  lh, sb["x_fm"][:, k, tq * VT + h2 * 512:tq * VT + h2 * 512 + 512],
                                  start=(k == 0), stop=(k == 7))
            S.inc(ii, "PE", f"v{idx}", 1)
        if stop < 3:
            return
        S.wait(t, "abase")
        ii = t.transpose(psb32[0:NCH, 128:256], sb["a_tm"], sb["ident_f"])
        S.inc(ii, "PE", "atr", 1)
        if stop == 3 and S3 < 2:
            return
        S.wait(t, "arbf")
        S.wait(t, "cbf")
        S.wait(t, "halo")
        S.wait(t, "pjbev")
        for tc in range(NC5):
            cb = 2048 + 512 * (tc % 2)
            if tc >= 2:
                S.wait(t, f"h4ev{tc - 2}")
            for j in range(3):
                ii = t.matmul(psf[0:4, cb:cb + 512],
                              sb["conv1_bf"][0:1, j * 4:(j + 1) * 4],
                              sb["arow_bf"][0:1, tc * 512 + j:tc * 512 + j + 512],
                              start=(j == 0), stop=(j == 2))
            S.inc(ii, "PE", f"c1_{tc}", 1)
        if stop == 3 and S3 < 3:
            return
        S.wait(t, "h4stats")
        S.wait(t, "consts")
        S.wait(t, "consts2")
        t.matmul(psb32[0:2, 52:53], sb["sel42"], sb["gtmp"][0:4, 0:1],
                 start=True, stop=True)
        ii = t.matmul(psb32[0:2, 53:54], sb["sel42"], sb["gtmp"][0:4, 1:2],
                      start=True, stop=True)
        S.inc(ii, "PE", "gnsum", 1)
        S.wait(t, "gnmr2")
        ii = t.matmul(psb32[0:4, 54:56], sb["sel24"], sb["gn_sc"][0:2, 48:50],
                      start=True, stop=True)
        S.inc(ii, "PE", "gnexp", 1)
        S.wait(t, "hrelu")
        for tc in range(NC5):
            cb = 2048 + 512 * (tc % 2)
            if tc >= 2:
                S.wait(t, f"y1ev{tc - 2}")
            for j in range(3):
                ii = t.matmul(psf[0:1, cb:cb + 512],
                              sb["conv2_bf"][0:4, j:j + 1],
                              sb["h4bf"][0:4, tc * 512 + j:tc * 512 + j + 512],
                              start=(j == 0), stop=(j == 2))
            S.inc(ii, "PE", f"c2_{tc}", 1)
        if stop == 3 and (S3 < 5 or S5 < 2):
            _skip_ch = True
        else:
            _skip_ch = False
        if not _skip_ch:
            S.wait(t, "mbf")
        if not _skip_ch:
            S.wait(t, "gnmr4")
            S.wait(t, "chw1")
            for oc in range(2):
                if oc >= 1:
                    S.wait(t, f"chrelu{oc - 1}")
                for k in range(8):
                    ii = t.matmul(psb32[:, 42 + oc:43 + oc],
                                  sb["chw1_sb"][:, (oc * 8 + k) * 128:(oc * 8 + k) * 128 + 128],
                                  sb["m_bf"][:, k:k + 1], start=(k == 0), stop=(k == 7))
                S.inc(ii, "PE", f"ch1_{oc}", 1)
            S.wait(t, "chrelu0")
            S.wait(t, "chrelu1")
            S.wait(t, "chw2")
            for oc in range(8):
                if oc >= 1:
                    S.wait(t, f"chexp{oc - 1}")
                for ih in range(2):
                    ii = t.matmul(psb32[:, 44 + oc:45 + oc],
                                  sb["chw2_sb"][:, (oc * 2 + ih) * 128:(oc * 2 + ih) * 128 + 128],
                                  sb["h1ch"][:, ih:ih + 1], start=(ih == 0), stop=(ih == 1))
                S.inc(ii, "PE", f"ch2_{oc}", 1)
        if stop == 3 and S3 < 4:
            return
        S.wait(t, "a1rd")
        if not _skip_ch:
            S.wait(t, "chexp7")
        ii = t.transpose(psb32[:, 8:8 + NCH], sb["arow32"],
                         sb["ident_f"][0:NCH, 0:NCH])
        S.inc(ii, "PE", "a1tr", 1)
        S.wait(t, "esum")
        ii = t.matmul(psb32[0:1, 40:41], sb["ones_c_f"], sb["rt4"][:, 0:1],
                      start=True, stop=True)
        S.inc(ii, "PE", "ssum", 1)
        S.wait(t, "invs")
        ii = t.matmul(psb32[:, 41:42], sb["ones_r_f"], sb["consts"][0:1, 3:4],
                      start=True, stop=True)
        S.inc(ii, "PE", "sbc", 1)
        if stop == 3 and S3 < 5:
            return
        S.wait(t, "invt")
        S.wait(t, "wcol")
        ii = t.matmul(psb32[:, 56:57], sb["ones_r_f"], sb["consts"][0:1, 2:3],
                      start=True, stop=True)
        S.inc(ii, "PE", "tbc", 1)
        if stop < 4:
            return
        for i in range(NCH):
            buf = i % 2
            if i < 2:
                S.wait(t, f"vev{NV - 1}")
                S.wait(t, "pp7")
            else:
                S.wait(t, f"ztt{i - 2}")
            for k in range(8):
                lh = sb["v_fm"][:, k, i * 128:(i + 1) * 128]
                t.matmul(psf[:, buf * 1024:buf * 1024 + 512], lh,
                         sb["pp_sb"][:, k, 0:512], start=(k == 0), stop=(k == 7))
                ii = t.matmul(psf[:, buf * 1024 + 512:buf * 1024 + 1024], lh,
                              sb["pp_sb"][:, k, 512:1024], start=(k == 0), stop=(k == 7))
            S.inc(ii, "PE", f"proj{i}", 1)
        if stop < 5:
            return

        def em_lnT(q, c):
            i = q * WCH + c
            S.wait(t, f"acc{q}_{E - 1}_7")
            S.wait(t, f"xl{i}")
            if i == 0:
                S.wait(t, "tcolev")
            else:
                S.wait(t, f"z2tt{i - 1}")
            for kb in range(8):
                ii = t.transpose(psb32[:, kb * 128:kb * 128 + 128],
                                 accb[q % 2][:, kb, c * 128:(c + 1) * 128],
                                 sb["ident_f"])
            S.inc(ii, "PE", f"lnT{i}", 1)

        if stop < 6:
            return
        for q in range(NQ):
            S.wait(t, f"x1q{q}")
            S.wait(t, f"x1qa{q}")
            if q == 0:
                S.wait(t, "wrbf")
                S.wait(t, "tcolev")
                S.wait(t, f"ztt{NCH - 1}")
            else:
                S.wait(t, f"sil{q - 1}_{E - 1}_7")
            for c in range(WCH):
                if c >= 1:
                    S.wait(t, f"lgev{q}_{c - 1}")
                for k in range(8):
                    ii = t.matmul(psf[:, 0:8], sb["x1q"][:, k, c * 128:(c + 1) * 128],
                                  sb["wr_bf"][:, k * 8:(k + 1) * 8],
                                  start=(k == 0), stop=(k == 7))
                S.inc(ii, "PE", f"lgq{q}_{c}", 1)
            S.wait(t, f"wroute{q}")
            for c in range(WCH):
                if c >= 1:
                    S.wait(t, f"wrev{q}_{c - 1}")
                ii = t.transpose(psf[0:8, 512:640],
                                 sb["wroute"][:, c * 8:(c + 1) * 8], sb["ident_f"])
                S.inc(ii, "PE", f"wrt{q}_{c}", 1)
            for e in range(E):
                j3 = (q * E + e) * 3
                if q >= 1 and e < WCH:
                    em_lnT(q - 1, e)
                S.wait(t, f"wrow{q}_{e}")
                if q == 0 and e == 0:
                    S.wait(t, f"x1bfst{NCH - 1}")
                else:
                    pq, pe = (q, e - 1) if e > 0 else (q - 1, E - 1)
                    S.wait(t, f"acc{pq}_{pe}_6")
                for h2 in range(W // 512):
                    ii = t.matmul(psf[:, 2048 + h2 * 512:2048 + h2 * 512 + 512],
                                  sb["ones_r_f"],
                                  sb["wrow0"][0:1, h2 * 512:h2 * 512 + 512],
                                  start=True, stop=True)
                S.inc(ii, "PE", f"wbc{q}_{e}", 1)
                for oi in range(8):
                    if oi == 0:
                        S.wait(t, f"slot{j3}")
                        S.wait(t, f"slot{j3 + 1}")
                        S.wait(t, f"x1q{q}")
                        S.wait(t, f"x1qa{q}")
                        if q == 0 and e == 0:
                            S.wait(t, f"ztt{NCH - 1}")
                    jg = (q * E + e) * 8 + oi
                    if jg >= 1:
                        pj = jg - 1
                        S.wait(t, f"sil{pj // (E * 8)}_{(pj // 8) % E}_{pj % 8}")
                    for k in range(8):
                        lh = sl[j3 % 3][:, (oi * 8 + k) * 128:(oi * 8 + k) * 128 + 128]
                        for h2 in range(W // 512):
                            ii = t.matmul(psf[:, h2 * 512:h2 * 512 + 512], lh,
                                          sb["x1q"][:, k, h2 * 512:h2 * 512 + 512],
                                          start=(k == 0), stop=(k == 7))
                    S.inc(ii, "PE", f"g{q}_{e}_{oi}", 1)
                    if oi == 7:
                        S.alias(f"mmdone{j3}", f"g{q}_{e}_7")
                    if jg >= 1:
                        pj = jg - 1
                        S.wait(t, f"upw{pj // (E * 8)}_{(pj // 8) % E}_{pj % 8}")
                    if oi == 0 and not (q == 0 and e == 0):
                        pq2, pe2 = (q, e - 1) if e > 0 else (q - 1, E - 1)
                        S.wait(t, f"acc{pq2}_{pe2}_7")
                    for k in range(8):
                        lh = sl[(j3 + 1) % 3][:, (oi * 8 + k) * 128:(oi * 8 + k) * 128 + 128]
                        for h2 in range(W // 512):
                            ii = t.matmul(psf[:, 1024 + h2 * 512:1024 + h2 * 512 + 512], lh,
                                          sb["x1q"][:, k, h2 * 512:h2 * 512 + 512],
                                          start=(k == 0), stop=(k == 7))
                    S.inc(ii, "PE", f"u{q}_{e}_{oi}", 1)
                    if oi == 7:
                        S.alias(f"mmdone{j3 + 1}", f"u{q}_{e}_7")
                        if e == E - 1:
                            S.alias(f"ffnGU{q}", f"u{q}_{e}_7")
                for dd in range(8):
                    dbase = 2048 if dd % 2 == 0 else 1024
                    if dd == 0:
                        S.wait(t, f"slot{j3 + 2}")
                        S.wait(t, f"hw{q}_{e}_7")
                        S.wait(t, f"wbcev{q}_{e}")
                    elif dd >= 2:
                        S.wait(t, f"acc{q}_{e}_{dd - 2}")
                    for ih in range(8):
                        lh = sl[(j3 + 2) % 3][:, (dd * 8 + ih) * 128:(dd * 8 + ih) * 128 + 128]
                        for h2 in range(W // 512):
                            ii = t.matmul(psf[:, dbase + h2 * 512:dbase + h2 * 512 + 512], lh,
                                          sb["hbuf"][:, ih, h2 * 512:h2 * 512 + 512],
                                          start=(ih == 0), stop=(ih == 7))
                    S.inc(ii, "PE", f"dn{q}_{e}_{dd}", 1)
                    if dd == 7:
                        S.alias(f"mmdone{j3 + 2}", f"dn{q}_{e}_7")
        for c in range(WCH):
            em_lnT(NQ - 1, c)

    def em_vector(v):
        def dr():
            v.drain()

        def newton_rsqrt(dst, src, p, w):
            t1 = sb["tn1"][0:p, 0:w]
            t2 = sb["tn2"][0:p, 0:w]
            v.tensor_scalar(t1.bitcast(I32), src.bitcast(I32), 1, None,
                            ALU.logical_shift_right)
            dr()
            v.tensor_scalar(t1.bitcast(I32), t1.bitcast(I32), -1, MAGIC,
                            ALU.mult, ALU.add)
            dr()
            last = None
            for it in range(3):
                ys = t1 if it == 0 else dst
                v.tensor_tensor(t2, ys, ys, ALU.mult)
                dr()
                v.tensor_tensor(t2, t2, src, ALU.mult)
                dr()
                v.tensor_scalar(t2, t2, -0.5, 1.5, ALU.mult, ALU.add)
                dr()
                last = v.tensor_tensor(dst, ys, t2, ALU.mult)
                dr()
            return last

        v.memset(sb["ones_r_bf"], 1.0)
        v.memset(sb["ones_r_f"], 1.0)
        v.memset(sb["ones_c_f"], 1.0)
        for i in range(NCH):
            S.wait(v, f"qk{i}")
            S.wait(v, f"qev{i}")
            v.bn_stats(sb["st6"][:, 0:6], psf[:, 1024:1536])
            v.bn_stats(sb["st6"][:, 6:12], psf[:, 1536:2048])
            ii = v.scalar_tensor_tensor(sb["qk_sc"], sb["q_sb"], 1.0,
                                        psf[:, 1024:2048], ALU.mult, ALU.mult,
                                        accum_out=sb["sqk"][:, i:i + 1])
            S.inc(ii, "DV", f"kst{i}", 1)
            dr()
            v.bn_aggr(sb["mv2"], sb["st6"])
            dr()
            v.tensor_copy(sb["muk"][:, i:i + 1], sb["mv2"][:, 0:1])
            v.tensor_copy(sb["vark"][:, i:i + 1], sb["mv2"][:, 1:2])
            v.bn_stats(sb["st6"][:, 0:6], sb["q_sb"][:, 0:512])
            v.bn_stats(sb["st6"][:, 6:12], sb["q_sb"][:, 512:1024])
            dr()
            v.bn_aggr(sb["mv2"], sb["st6"])
            dr()
            v.tensor_copy(sb["muq"][:, i:i + 1], sb["mv2"][:, 0:1])
            ii = v.tensor_copy(sb["varq"][:, i:i + 1], sb["mv2"][:, 1:2])
            S.inc(ii, "DV", f"qst{i}", 1)
            dr()
        if stop < 2:
            return
        S.wait(v, "consts")
        S.wait(v, "consts2")
        v.reciprocal(sb["consts"][0:1, 2:3], sb["scal4_sb"][0:1, 2:3])
        ii = S.inc(v.tensor_scalar(sb["negb2"], sb["chb2_sb"], -1.0, None, ALU.mult),
                   "DV", "negb2", 1)
        S.alias("invt", "negb2")
        dr()
        S.wait(v, "ccbc")
        v.tensor_copy(sb["consts"][:, 0:1], psb32[:, 57:58])
        ii = v.tensor_copy(sb["consts"][:, 1:2], psb32[:, 58:59])
        S.inc(ii, "DV", "ccev", 1)
        dr()
        S.wait(v, "pjb")
        ii = v.tensor_copy(sb["projb_bc"], psf[:, 2048:3072])
        S.inc(ii, "DV", "pjbev", 1)
        if stop < 3:
            return
        c1c = sb["consts"][:, 0:1]
        c2c = sb["consts"][:, 1:2]
        t1 = sb["rt4"]
        t2 = sb["rt5"]
        K1023 = 1024.0 / 1023.0
        v.tensor_tensor(t1, sb["muq"], sb["muk"], ALU.mult)
        dr()
        v.tensor_scalar(t2, sb["sqk"], 1.0 / 1024.0, None, ALU.mult)
        dr()
        v.tensor_tensor(t2, t2, t1, ALU.subtract)
        dr()
        v.tensor_scalar(t2, t2, 2.0 * K1023, c2c, ALU.mult, ALU.add)
        dr()
        v.tensor_scalar(t1, t1, 2.0, c1c, ALU.mult, ALU.add)
        dr()
        v.tensor_tensor(t1, t1, t2, ALU.mult)
        dr()
        v.tensor_tensor(t2, sb["varq"], sb["vark"], ALU.add)
        dr()
        v.tensor_scalar(t2, t2, K1023, c2c, ALU.mult, ALU.add)
        dr()
        v.tensor_tensor(sb["a_tm"], sb["muq"], sb["muq"], ALU.mult)
        dr()
        v.tensor_tensor(sb["qk_sc"][:, 0:NCH], sb["muk"], sb["muk"], ALU.mult)
        dr()
        v.tensor_tensor(sb["a_tm"], sb["a_tm"], sb["qk_sc"][:, 0:NCH], ALU.add)
        dr()
        v.tensor_scalar(sb["a_tm"], sb["a_tm"], c1c, None, ALU.add)
        dr()
        v.tensor_tensor(t2, sb["a_tm"], t2, ALU.mult)
        dr()
        v.tensor_scalar(t2, t2, 1e-7, None, ALU.add)
        dr()
        v.reciprocal(sb["a_tm"], t2)
        dr()
        v.tensor_tensor(t1, t1, sb["a_tm"], ALU.mult)
        dr()
        ii = v.tensor_tensor(sb["a_tm"], t1, t1, ALU.mult)
        S.inc(ii, "DV", "abase", 1)
        if stop == 3 and S3 < 2:
            return
        for idx in range(NV):
            oc, tq = divmod(idx, NVT)
            S.wait(v, f"v{idx}")
            buf = idx % 2
            ii = v.tensor_copy(sb["v_fm"][:, oc, tq * VT:(tq + 1) * VT],
                               psf[:, buf * 1024:buf * 1024 + VT])
            S.inc(ii, "DV", f"vev{idx}", 1)
        dr()
        v.reduce_sum(sb["m_fm"].rearrange("p (a b) -> p a b", b=1), sb["x_fm"],
                     axis=AX)
        dr()
        ii = v.tensor_scalar(sb["m_fm"], sb["m_fm"], 1.0 / N, None, ALU.mult)
        S.inc(ii, "DV", "mfm", 1)
        dr()
        ii = v.tensor_copy(sb["m_bf"], sb["m_fm"])
        S.inc(ii, "DV", "mbf", 1)
        S.wait(v, "atr")
        ii = v.tensor_copy(sb["arow32"], psb32[0:NCH, 128:256])
        S.inc(ii, "DV", "arowev", 1)
        S.wait(v, "consts")
        v.tensor_copy(sb["conv1_bf"], sb["conv1_sb"])
        ii = v.tensor_copy(sb["conv2_bf"], sb["conv2_sb"])
        S.inc(ii, "DV", "cbf", 1)
        S.wait(v, f"tr{NCH - 1}")
        S.wait(v, f"qk{NCH - 1}")
        v.memset(sb["arow"][0:1, 0:1], 0.0)
        v.memset(sb["arow"][0:1, N + 1:N + 2], 0.0)
        v.memset(sb["arow_bf"][0:1, 0:1], 0.0)
        v.memset(sb["arow_bf"][0:1, N + 1:N + 2], 0.0)
        v.memset(sb["h4"][0:4, 0:1], 0.0)
        v.memset(sb["h4"][0:4, N + 1:N + 2], 0.0)
        v.memset(sb["h4bf"][0:4, 0:1], 0.0)
        ii = v.memset(sb["h4bf"][0:4, N + 1:N + 2], 0.0)
        S.inc(ii, "DV", "halo", 1)
        for tc in range(NC5):
            cb = 2048 + 512 * (tc % 2)
            S.wait(v, f"c1_{tc}")
            ii = v.tensor_copy(sb["h4"][0:4, 1 + tc * 512:1 + tc * 512 + 512],
                               psf[0:4, cb:cb + 512])
            S.inc(ii, "DV", f"h4ev{tc}", 1)
        dr()
        if stop == 3 and S3 < 3:
            return
        for b8 in range(NC5):
            v.bn_stats(sb["gn_sc"][0:4, b8 * 6:b8 * 6 + 6],
                       sb["h4"][0:4, 1 + b8 * 512:1 + b8 * 512 + 512])
        dr()
        v.bn_aggr(sb["gn_sc"][0:4, 48:50], sb["gn_sc"][0:4, 0:6 * NC5])
        dr()
        v.tensor_copy(sb["gtmp"][0:4, 0:1], sb["gn_sc"][0:4, 48:49])
        v.tensor_tensor(sb["gtmp"][0:4, 2:3], sb["gn_sc"][0:4, 48:49],
                        sb["gn_sc"][0:4, 48:49], ALU.mult)
        dr()
        ii = v.tensor_tensor(sb["gtmp"][0:4, 1:2], sb["gn_sc"][0:4, 49:50],
                             sb["gtmp"][0:4, 2:3], ALU.add)
        S.inc(ii, "DV", "h4stats", 1)
        S.wait(v, "gnsum")
        v.tensor_scalar(sb["gn_sc"][0:2, 48:49], psb32[0:2, 52:53], 0.5, None,
                        ALU.mult)
        v.tensor_scalar(sb["gn_sc"][0:2, 50:51], psb32[0:2, 53:54], 0.5, None,
                        ALU.mult)
        dr()
        v.tensor_tensor(sb["gn_sc"][0:2, 51:52], sb["gn_sc"][0:2, 48:49],
                        sb["gn_sc"][0:2, 48:49], ALU.mult)
        dr()
        v.tensor_tensor(sb["gn_sc"][0:2, 50:51], sb["gn_sc"][0:2, 50:51],
                        sb["gn_sc"][0:2, 51:52], ALU.subtract)
        dr()
        v.tensor_scalar(sb["gn_sc"][0:2, 50:51], sb["gn_sc"][0:2, 50:51], GN_EPS,
                        None, ALU.add)
        dr()
        ii = newton_rsqrt(sb["gn_sc"][0:2, 49:50], sb["gn_sc"][0:2, 50:51], 2, 1)
        S.inc(ii, "DV", "gnmr2", 1)
        S.wait(v, "gnexp")
        ii = v.tensor_copy(sb["gtmp"][0:4, 4:6], psb32[0:4, 54:56])
        S.inc(ii, "DV", "gnmr4", 1)
        dr()
        ii = v.tensor_scalar(sb["h4"][0:4, 1:N + 1], sb["h4"][0:4, 1:N + 1],
                             sb["gtmp"][0:4, 4:5], sb["gtmp"][0:4, 5:6],
                             ALU.subtract, ALU.mult)
        S.inc(ii, "DV", "gnapply", 1)
        dr()
        for tc in range(NC5):
            cb = 2048 + 512 * (tc % 2)
            S.wait(v, f"c2_{tc}")
            if tc == 0:
                S.wait(v, "arrd")
            v.tensor_scalar(sb["zbuf"][0:1, 0:512], psf[0:1, cb:cb + 512],
                            sb["scal4_sb"][0:1, 3:4], None, ALU.add)
            dr()
            ii = v.tensor_tensor(sb["arow"][0:1, 1 + tc * 512:1 + tc * 512 + 512],
                                 sb["arow"][0:1, 1 + tc * 512:1 + tc * 512 + 512],
                                 sb["zbuf"][0:1, 0:512], ALU.add)
            S.inc(ii, "DV", f"y1ev{tc}", 1)
            dr()
        S.alias("a1ev", f"y1ev{NC5 - 1}")
        if stop == 3 and S3 < 4:
            return
        S.wait(v, "a1tr")
        ii = v.tensor_copy(sb["a_tm"], psb32[:, 8:8 + NCH])
        S.inc(ii, "DV", "a1tm", 1)
        S.wait(v, "siga")
        v.tensor_scalar(sb["rt4"], sb["rt4"], 1.0, None, ALU.add)
        dr()
        ii = v.reciprocal(sb["a_tm"], sb["rt4"])
        S.inc(ii, "DV", "sigtm", 1)
        S.wait(v, "expa")
        ii = v.reduce_sum(sb["rt4"][:, 0:1], sb["rt5"], axis=AX)
        S.inc(ii, "DV", "esum", 1)
        S.wait(v, "ssum")
        ii = v.reciprocal(sb["consts"][0:1, 3:4], psb32[0:1, 40:41])
        S.inc(ii, "DV", "invs", 1)
        S.wait(v, "sbc")
        v.tensor_copy(sb["consts"][:, 3:4], psb32[:, 41:42])
        dr()
        ii = v.tensor_scalar(sb["w_col"], sb["rt5"], sb["consts"][:, 3:4], None,
                             ALU.mult)
        S.inc(ii, "DV", "wcol", 1)
        if stop == 3 and S3 < 5:
            return
        S.wait(v, "tbc")
        ii = v.tensor_copy(sb["consts"][:, 2:3], psb32[:, 56:57])
        S.inc(ii, "DV", "tcolev", 1)
        dr()
        ii = v.tensor_scalar(sb["wr_bf"], sb["wr_f"], sb["consts"][:, 2:3], None,
                             ALU.mult)
        S.inc(ii, "DV", "wrbf", 1)
        if stop == 3 and S5 < 2:
            return
        for oc in range(8):
            S.wait(v, f"chexp{oc}")
        v.tensor_scalar(sb["ch_w"], sb["ch_w"], 1.0, None, ALU.add)
        dr()
        ii = v.reciprocal(sb["ch_w"], sb["ch_w"])
        S.inc(ii, "DV", "chw", 1)
        dr()
        if stop == 3 and S5 < 3:
            return
        S.wait(v, "pjall")
        for k in range(8):
            ii = v.tensor_scalar(sb["pp_sb"][:, k, :], sb["pstg32"][:, k, :],
                                 sb["ch_w"][:, k:k + 1], None, ALU.mult)
            S.inc(ii, "DV", f"ppk{k}", 1)
            dr()
        S.alias("pp7", "ppk7")
        if stop < 4:
            return
        if stop < 4:
            return
        for i in range(NCH):
            buf = i % 2
            S.wait(v, f"proj{i}")
            S.wait(v, f"xb{i}")
            ii = v.scalar_tensor_tensor(sb["zbuf"], psf[:, buf * 1024:buf * 1024 + 1024],
                                        sb["w_col"][:, i:i + 1], xfb[buf],
                                        ALU.mult, ALU.add)
            S.inc(ii, "DV", f"ztt{i}", 1)
            dr()
            v.tensor_tensor(sb["zbuf"], sb["zbuf"], sb["projb_bc"], ALU.add)
            dr()
            v.bn_stats(sb["st6"][:, 0:6], sb["zbuf"][:, 0:512])
            v.bn_stats(sb["st6"][:, 6:12], sb["zbuf"][:, 512:1024])
            dr()
            v.bn_aggr(sb["mv2"], sb["st6"])
            dr()
            ii = v.tensor_scalar(sb["mv2"][:, 1:2], sb["mv2"][:, 1:2], EPS, None,
                                 ALU.add)
            S.inc(ii, "DV", f"vb{i}", 1)
            dr()
            if SUB < 1:
                continue
            if i >= 2:
                S.wait(v, f"x1st{i - 2}")
            S.wait(v, f"rstd{i}")
            ii = v.tensor_scalar(x1bb[i % 2], sb["zbuf"], sb["mv2"][:, 0:1],
                                 sb["tn2"][:, 3:4], ALU.subtract, ALU.mult)
            S.inc(ii, "DV", f"x1{i}", 1)
            if i >= 2:
                S.wait(v, f"x1bfst{i - 2}")
            dr()
            ii = v.tensor_copy(hibb[i % 2], x1bb[i % 2])
            S.inc(ii, "DV", f"xbf{i}", 1)
            dr()
        def em_lnV(q, c):
            i = q * WCH + c
            S.wait(v, f"lnT{i}")
            S.wait(v, f"xl{i}")
            ii = v.tensor_tensor(sb["zbuf"], sb["xlb"], psb32[:, 0:1024], ALU.add)
            S.inc(ii, "DV", f"z2tt{i}", 1)
            dr()
            v.bn_stats(sb["st6"][:, 0:6], sb["zbuf"][:, 0:512])
            v.bn_stats(sb["st6"][:, 6:12], sb["zbuf"][:, 512:1024])
            dr()
            v.bn_aggr(sb["mv2"], sb["st6"])
            dr()
            v.tensor_scalar(sb["mv2"][:, 1:2], sb["mv2"][:, 1:2], EPS, None,
                            ALU.add)
            dr()
            newton_rsqrt(sb["tn1"][:, 2:3], sb["mv2"][:, 1:2], 128, 1)
            if i >= 1:
                S.wait(v, f"outst{i - 1}")
            ii = v.tensor_scalar(sb["obb"], sb["zbuf"], sb["mv2"][:, 0:1],
                                 sb["tn1"][:, 2:3], ALU.subtract, ALU.mult)
            S.inc(ii, "DV", f"out{i}", 1)
            dr()

        if stop < 6:
            return

        NEQ = WCH * 8

        def g3(ap):
            return ap.rearrange("p (a b) -> p a b", b=8)

        def bc3(ap):
            return ap.rearrange("p (a b) -> p a b", b=1).broadcast_to([128, WCH, 8])

        def em_routeV(q):
            for c in range(WCH):
                S.wait(v, f"lgq{q}_{c}")
                ii = v.tensor_copy(sb["rt1"][:, c * 8:(c + 1) * 8], psf[:, 0:8])
                S.inc(ii, "DV", f"lgev{q}_{c}", 1)
                dr()
            S.alias(f"rlg{q}", f"lgev{q}_{WCH - 1}")
            S.wait(v, f"rsig{q}")
            v.tensor_scalar(sb["rt2"][:, 0:NEQ], sb["rt2"][:, 0:NEQ], 1.0, None,
                            ALU.add)
            dr()
            v.reciprocal(sb["rt1"][:, 0:NEQ], sb["rt2"][:, 0:NEQ])
            dr()
            v.tensor_scalar(sb["rt2"][:, 0:NEQ], sb["rt1"][:, 0:NEQ], G_LOWER,
                            None, ALU.is_ge)
            dr()
            v.tensor_scalar(sb["wroute"][:, 0:NEQ], sb["rt1"][:, 0:NEQ], G_UPPER,
                            None, ALU.is_le)
            dr()
            v.tensor_tensor(sb["wroute"][:, 0:NEQ], sb["wroute"][:, 0:NEQ],
                            sb["rt2"][:, 0:NEQ], ALU.mult)
            dr()
            v.tensor_scalar(sb["rt1"][:, 0:NEQ], sb["rt1"][:, 0:NEQ], G_CENTER,
                            None, ALU.subtract)
            dr()
            if q == 0:
                S.wait(v, f"x1bfst{NCH - 1}")
            v.tensor_scalar(sb["rt3"][:, 0:NEQ], sb["rt1"][:, 0:NEQ], -1.0, None,
                            ALU.mult)
            dr()
            ii = v.tensor_tensor(sb["rt1"][:, 0:NEQ], sb["rt1"][:, 0:NEQ],
                                 sb["rt3"][:, 0:NEQ], ALU.max)
            S.inc(ii, "DV", f"rdist{q}", 1)
            S.wait(v, f"rexp1{q}")
            S.wait(v, f"rexp2{q}")
            v.tensor_tensor(sb["rt2"][:, 0:NEQ], sb["rt2"][:, 0:NEQ],
                            sb["wroute"][:, 0:NEQ], ALU.mult)
            dr()
            v.reduce_sum(sb["rt4"][:, 0:WCH].rearrange("p (a b) -> p a b", b=1),
                         g3(sb["rt2"][:, 0:NEQ]), axis=AX)
            dr()
            v.tensor_scalar(sb["rt5"][:, 0:WCH], sb["rt4"][:, 0:WCH], 1e-8, None,
                            ALU.is_lt)
            dr()
            v.reduce_max(sb["rt4"][:, 0:WCH].rearrange("p (a b) -> p a b", b=1),
                         g3(sb["rt3"][:, 0:NEQ]), axis=AX)
            dr()
            v.tensor_tensor(g3(sb["rt1"][:, 0:NEQ]), g3(sb["rt3"][:, 0:NEQ]),
                            bc3(sb["rt4"][:, 0:WCH]), ALU.is_ge)
            dr()
            v.tensor_scalar(sb["rt1"][:, 0:NEQ], sb["rt1"][:, 0:NEQ], -1e30, None,
                            ALU.mult)
            dr()
            v.tensor_tensor(sb["rt1"][:, 0:NEQ], sb["rt1"][:, 0:NEQ],
                            sb["rt3"][:, 0:NEQ], ALU.add)
            dr()
            v.reduce_max(sb["rt4"][:, 0:WCH].rearrange("p (a b) -> p a b", b=1),
                         g3(sb["rt1"][:, 0:NEQ]), axis=AX)
            dr()
            v.tensor_tensor(g3(sb["rt1"][:, 0:NEQ]), g3(sb["rt3"][:, 0:NEQ]),
                            bc3(sb["rt4"][:, 0:WCH]), ALU.is_ge)
            dr()
            v.tensor_tensor(sb["rt3"][:, 0:NEQ], sb["rt3"][:, 0:NEQ],
                            sb["rt1"][:, 0:NEQ], ALU.mult)
            dr()
            v.reduce_sum(sb["rt4"][:, 0:WCH].rearrange("p (a b) -> p a b", b=1),
                         g3(sb["rt3"][:, 0:NEQ]), axis=AX)
            dr()
            v.tensor_scalar(sb["rt4"][:, 0:WCH], sb["rt4"][:, 0:WCH], 1e-8, None,
                            ALU.max)
            dr()
            v.reciprocal(sb["rt4"][:, 0:WCH], sb["rt4"][:, 0:WCH])
            dr()
            v.tensor_tensor(g3(sb["rt3"][:, 0:NEQ]), g3(sb["rt3"][:, 0:NEQ]),
                            bc3(sb["rt4"][:, 0:WCH]), ALU.mult)
            dr()
            v.tensor_tensor(sb["rt3"][:, 0:NEQ], sb["rt3"][:, 0:NEQ],
                            sb["rt2"][:, 0:NEQ], ALU.subtract)
            dr()
            v.tensor_tensor(g3(sb["rt3"][:, 0:NEQ]), g3(sb["rt3"][:, 0:NEQ]),
                            bc3(sb["rt5"][:, 0:WCH]), ALU.mult)
            dr()
            v.tensor_tensor(sb["rt2"][:, 0:NEQ], sb["rt2"][:, 0:NEQ],
                            sb["rt3"][:, 0:NEQ], ALU.add)
            dr()
            v.reduce_sum(sb["rt4"][:, 0:WCH].rearrange("p (a b) -> p a b", b=1),
                         g3(sb["rt2"][:, 0:NEQ]), axis=AX)
            dr()
            v.tensor_scalar(sb["rt4"][:, 0:WCH], sb["rt4"][:, 0:WCH], 1e-8, None,
                            ALU.max)
            dr()
            v.reciprocal(sb["rt4"][:, 0:WCH], sb["rt4"][:, 0:WCH])
            dr()
            ii = v.tensor_tensor(g3(sb["wroute"][:, 0:NEQ]),
                                 g3(sb["rt2"][:, 0:NEQ]),
                                 bc3(sb["rt4"][:, 0:WCH]), ALU.mult)
            S.inc(ii, "DV", f"wroute{q}", 1)
            dr()
            for c in range(WCH):
                S.wait(v, f"wrt{q}_{c}")
                if c == 0 and q >= 1:
                    S.wait(v, f"wrst{q - 1}")
                ii = v.tensor_copy(sb["w_rows"][0:8, c * 128:(c + 1) * 128],
                                   psf[0:8, 512:640])
                S.inc(ii, "DV", f"wrev{q}_{c}", 1)
                dr()

        for q in range(NQ):
            em_routeV(q)
            for e in range(E):
                if q >= 1 and e < WCH:
                    em_lnV(q - 1, e)
                S.wait(v, f"wbc{q}_{e}")
                ii = v.tensor_copy(sb["wbc"], psf[:, 2048:2048 + W])
                S.inc(ii, "DV", f"wbcev{q}_{e}", 1)
                dr()
                for oi in range(8):
                    S.wait(v, f"u{q}_{e}_{oi}")
                    ii = v.tensor_tensor(upwb[oi % 2], psf[:, 1024:1024 + W],
                                         sb["wbc"], ALU.mult)
                    S.inc(ii, "DV", f"upw{q}_{e}_{oi}", 1)
                    dr()
                    S.wait(v, f"sil{q}_{e}_{oi}")
                    if oi == 0 and not (q == 0 and e == 0):
                        pq, pe = (q, e - 1) if e > 0 else (q - 1, E - 1)
                        S.wait(v, f"dn{pq}_{pe}_7")
                    ii = v.tensor_tensor(sb["hbuf"][:, oi, :], upwb[oi % 2],
                                         silb[oi % 2], ALU.mult)
                    S.inc(ii, "DV", f"hw{q}_{e}_{oi}", 1)
                for dd in range(8):
                    dbase = 2048 if dd % 2 == 0 else 1024
                    S.wait(v, f"dn{q}_{e}_{dd}")
                    if e == 0:
                        if q >= 2 and dd == 0:
                            S.wait(v, f"lnT{(q - 2) * WCH + WCH - 1}")
                        ii = v.tensor_copy(accb[q % 2][:, dd, :],
                                           psf[:, dbase:dbase + W])
                    else:
                        ii = v.tensor_tensor(accb[q % 2][:, dd, :],
                                             accb[q % 2][:, dd, :],
                                             psf[:, dbase:dbase + W], ALU.add)
                    S.inc(ii, "DV", f"acc{q}_{e}_{dd}", 1)
                    dr()
        for c in range(WCH):
            em_lnV(NQ - 1, c)

    def run_all(gp, sy, sc, te, ve):
        em_gpsimd(gp)
        em_sync(sy)
        em_scalar(sc)
        em_tensor(te)
        em_vector(ve)

    S.dry = True
    px = EngProxy()
    run_all(px, px, px, px, px)
    S.dry = False
    S.reset()
    with nc.Block() as block:
        @block.gpsimd
        def _(g):
            em_gpsimd(g)

        @block.sync
        def _(sy):
            em_sync(sy)

        @block.scalar
        def _(sc):
            em_scalar(sc)

        @block.tensor
        def _(te):
            em_tensor(te)

        @block.vector
        def _(ve):
            em_vector(ve)
    stack.close()
    return nc


def prep_host(inputs):
    c = np.ascontiguousarray
    qkv = np.asarray(inputs["L0_attn_qkv_w"], np.float32)
    h = {}
    h["wqk"] = c(qkv.reshape(8, 128, 3072)[:, :, :2048].transpose(1, 0, 2))
    h["wv"] = c(qkv[:, 2048:3072].reshape(8, 128, 8, 128)
                .transpose(1, 2, 0, 3).reshape(128, 8192))
    h["projw"] = c(np.asarray(inputs["L0_attn_proj_w"], np.float32)
                   .reshape(8, 128, 1024).transpose(1, 0, 2))
    h["projb"] = c(np.asarray(inputs["L0_attn_proj_b"], np.float32).reshape(1, 1024))
    h["wr"] = c(np.asarray(inputs["L0_ffn_Wr"], np.float32)
                .reshape(8, 128, 8).transpose(1, 0, 2).reshape(128, 64))
    h["chw1"] = c(np.asarray(inputs["L0_attn_ch_w1"], np.float32)
                  .reshape(8, 128, 2, 128).transpose(1, 2, 0, 3).reshape(128, 2048))
    h["chw2"] = c(np.asarray(inputs["L0_attn_ch_w2"], np.float32)
                  .reshape(2, 128, 8, 128).transpose(1, 2, 0, 3).reshape(128, 2048))
    h["chb1"] = c(np.asarray(inputs["L0_attn_ch_b1"], np.float32).reshape(2, 128).T)
    h["chb2"] = c(np.asarray(inputs["L0_attn_ch_b2"], np.float32).reshape(8, 128).T)
    h["conv1"] = c(np.asarray(inputs["L0_attn_conv1_w"], np.float32)[:, 0, :]
                   .T.reshape(1, 12))
    h["conv2"] = c(np.asarray(inputs["L0_attn_conv2_w"], np.float32)[0])
    h["gng"] = c(np.asarray(inputs["L0_attn_gn_g"], np.float32).reshape(4, 1))
    h["gnb"] = c(np.asarray(inputs["L0_attn_gn_b"], np.float32).reshape(4, 1))
    h["scal4"] = np.stack([np.float32(np.asarray(inputs["L0_attn_c1"]).reshape(())),
                           np.float32(np.asarray(inputs["L0_attn_c2"]).reshape(())),
                           np.float32(np.asarray(inputs["L0_ffn_temperature"]).reshape(-1)[0]),
                           np.float32(np.asarray(inputs["L0_attn_conv2_b"]).reshape(-1)[0])]
                          ).reshape(1, 4).astype(np.float32)
    h["eye"] = np.eye(128, dtype=np.float32)
    h["sel42d"] = np.array([[1, 0], [1, 0], [0, 1], [0, 1]], np.float32)
    h["sel24d"] = np.array([[1, 1, 0, 0], [0, 0, 1, 1]], np.float32)
    for nm, key in (("wg", "L0_ffn_Wg"), ("wu", "L0_ffn_Wu"), ("wd", "L0_ffn_Wd")):
        w = np.asarray(inputs[key], np.float32)
        h[nm] = c(w.reshape(E, 8, 128, 8, 128).transpose(0, 2, 3, 1, 4)
                  .reshape(E, 128, 8192))
    return h


def kernel(**inputs):
    x_all = np.ascontiguousarray(np.asarray(inputs["L0_attn_x"], np.float32))
    N = x_all.shape[1]
    h = prep_host(inputs)
    nc = build_program(N)
    in_maps = []
    for b in range(x_all.shape[0]):
        m = dict(h)
        m["x"] = np.ascontiguousarray(x_all[b])
        in_maps.append(m)
    res = run_bass_kernel_spmd(nc, in_maps, core_ids=list(range(len(in_maps))))
    return np.stack([r["out"] for r in res.results], axis=0)
